# version 28
# baseline (speedup 1.0000x reference)
import numpy as np
import ml_dtypes
from contextlib import ExitStack

import concourse.bass as bass
import concourse.mybir as mybir
from concourse.bass_utils import run_bass_kernel_spmd

F32 = mybir.dt.float32
BF16 = mybir.dt.bfloat16
ALU = mybir.AluOpType
AF = mybir.ActivationFunctionType
AX = mybir.AxisListType

D = 1024
T = 2048
KC = 8
FC = 16
E = 8
F = 2048
H = 16
HD = 64
GRP = 256
NGRP = T // GRP
HALF = 1024
NCORE = 8
BPC = 4
SQ5 = float(np.sqrt(0.5))

ENGINES = ("sync", "gpsimd", "tensor", "vector", "scalar")


class Sem:
    def __init__(self, ctx, nc, name):
        self.h = ctx.enter_context(nc.semaphore(name))
        self.c = 0


class Emitter:

    def __init__(self, block):
        self.block = block
        self.q = {e: [] for e in ENGINES}

    def add(self, eng, fn):
        self.q[eng].append(fn)

    def flush(self):
        for eng in ENGINES:
            fns = self.q[eng]
            if not fns:
                continue

            def run(h, fns=fns):
                for f in fns:
                    f(h)

            getattr(self.block, eng)(run)
        self.q = {e: [] for e in ENGINES}


def build(n_batches=BPC, sublayers=(0, 1, 2), FFN_NEXP=E, FFN_PHASE=3,
          ATTN_PHASE=4, ATTN_NGRP=NGRP, ATTN_SUB=4):
    nc = bass.Bass("TRN2", target_bir_lowering=False)

    xT = nc.dram_tensor("xT", [BPC, D, T], F32, kind="ExternalInput").ap()
    out_d = nc.dram_tensor("outT", [BPC, D, T], F32, kind="ExternalOutput").ap()
    wu_d = nc.dram_tensor("wu_p", [E, FC, 128, KC, 128], F32, kind="ExternalInput").ap()
    wd_d = nc.dram_tensor("wd_p", [E, FC, 128, D], F32, kind="ExternalInput").ap()
    wq_d = nc.dram_tensor("wq_m", [KC, 128, D], F32, kind="ExternalInput").ap()
    wk_d = nc.dram_tensor("wk_m", [KC, 128, D], F32, kind="ExternalInput").ap()
    wo_d = nc.dram_tensor("wo_m", [KC, 128, D], F32, kind="ExternalInput").ap()
    wkk_d = nc.dram_tensor("wk_k", [KC, 128, D], F32, kind="ExternalInput").ap()
    wvk_d = nc.dram_tensor("wv_k", [KC, 128, D], F32, kind="ExternalInput").ap()
    gw_d = nc.dram_tensor("gw_p", [KC, 128, E], F32, kind="ExternalInput").ap()
    bias_d = nc.dram_tensor("ebias", [1, E], F32, kind="ExternalInput").ap()
    gam_d = nc.dram_tensor("gam", [1, H], F32, kind="ExternalInput").ap()
    etri_d = nc.dram_tensor("etri", [128, 128], BF16, kind="ExternalInput").ap()
    tri_d = nc.dram_tensor("tri", [128, 128], BF16, kind="ExternalInput").ap()
    erow_d = nc.dram_tensor("erow", [128, 128], BF16, kind="ExternalInput").ap()
    ecol_d = nc.dram_tensor("ecol", [128, 1], F32, kind="ExternalInput").ap()
    idn_d = nc.dram_tensor("idn", [128, 128], F32, kind="ExternalInput").ap()
    ones1_d = nc.dram_tensor("ones1", [1, 128], F32, kind="ExternalInput").ap()
    onesb_d = nc.dram_tensor("onesb", [128, 1], BF16, kind="ExternalInput").ap()
    idnb_d = nc.dram_tensor("idnb", [128, 128], BF16, kind="ExternalInput").ap()
    sel_d = nc.dram_tensor("sel_p", [E, E, 128], BF16, kind="ExternalInput").ap()
    cst_d = nc.dram_tensor("cvals", [1, 8], F32, kind="ExternalInput").ap()
    lnrow_d = nc.dram_tensor("lnrow", [1, 2048], BF16, kind="Internal").ap()
    wvb_d = nc.dram_tensor("wv_b16", [KC, 128, D], BF16, kind="Internal").ap()
    wob_d = nc.dram_tensor("wo_b16", [KC, 128, D], BF16, kind="Internal").ap()
    wqb_d = nc.dram_tensor("wq_b16", [KC, 128, D], BF16, kind="Internal").ap()
    wkb_d = nc.dram_tensor("wk_b16", [KC, 128, D], BF16, kind="Internal").ap()
    wub_d = nc.dram_tensor("wu_b16", [E, FC, 128, KC, 128], BF16,
                           kind="Internal").ap()
    wdb_d = nc.dram_tensor("wd_b16", [E, FC, 128, D], BF16, kind="Internal").ap()

    ctx = ExitStack()
    with ctx:
        sbt = nc.sbuf_tensor
        res = ctx.enter_context(sbt("res", [128, KC * T], F32))
        lnT = ctx.enter_context(sbt("lnT", [128, KC * T], BF16))
        hbuf = ctx.enter_context(sbt("hbuf", [128, FC * HALF], BF16))
        wdbuf = ctx.enter_context(sbt("wdbuf", [128, FC * D], BF16))
        wst = ctx.enter_context(sbt("wst", [128, 2 * 2048], BF16))
        sbf = ctx.enter_context(sbt("sbf", [128, 4096], BF16))
        gsc = ctx.enter_context(sbt("gsc", [128, 256], F32))
        rowb = ctx.enter_context(sbt("rowb", [8, 2048], F32))
        mbc = sbf[:, 3072:4096]
        mT = ctx.enter_context(sbt("mT", [8, T], BF16))
        gmx = ctx.enter_context(sbt("gmx", [128, 32], F32))
        dct = ctx.enter_context(sbt("dct", [128, H * 128], BF16))
        grw = ctx.enter_context(sbt("grw", [128, KC * 128], BF16))
        kdc = ctx.enter_context(sbt("kdc", [128, D], BF16))
        gcs = ctx.enter_context(sbt("gcs", [128, 512], BF16))
        lgb = ctx.enter_context(sbt("lgb", [128, H], F32))
        bcb = ctx.enter_context(sbt("bcb", [128, E], F32))
        expc = ctx.enter_context(sbt("expc", [128, 1], F32))
        idn_s = ctx.enter_context(sbt("idn_s", [128, 128], F32))
        idnb_s = ctx.enter_context(sbt("idnb_s", [128, 128], BF16))
        ones1 = ctx.enter_context(sbt("ones1f", [1, 128], F32))
        onesb = ctx.enter_context(sbt("onesbf", [128, 1], BF16))
        sel_s = ctx.enter_context(sbt("sel_s", [E, E * 128], BF16))
        gw_s = ctx.enter_context(sbt("gw_s", [128, KC * E], BF16))
        cst = ctx.enter_context(sbt("cst_s", [128, 8], F32))
        tri_s = ctx.enter_context(sbt("tri_s", [128, 128], BF16))
        etri_s = ctx.enter_context(sbt("etri_s", [128, 128], BF16))
        erow_s = ctx.enter_context(sbt("erow_s", [128, 128], BF16))
        tf32 = ctx.enter_context(sbt("tf32", [128, 128], F32))
        S_all = ctx.enter_context(sbt("S_all", [128, 512], F32))
        S_bf = ctx.enter_context(sbt("S_bf", [128, 512], BF16))

        P0 = ctx.enter_context(nc.psum_tensor("P0", [128, 1024], F32))
        P1 = ctx.enter_context(nc.psum_tensor("P1", [128, 1024], F32))
        P2 = ctx.enter_context(nc.psum_tensor("P2", [128, 1024], F32))
        P3 = ctx.enter_context(nc.psum_tensor("P3", [128, 1024], F32))

        s_x = Sem(ctx, nc, "s_x")
        s_c = Sem(ctx, nc, "s_c")
        s_g = Sem(ctx, nc, "s_g")
        s_wd = Sem(ctx, nc, "s_wd")
        s_wk = Sem(ctx, nc, "s_wk")
        s_w0 = Sem(ctx, nc, "s_w0")
        s_w1 = Sem(ctx, nc, "s_w1")
        s_w2 = Sem(ctx, nc, "s_w2")
        s_ch = [Sem(ctx, nc, f"s_ch{k}") for k in range(KC)]
        s_pc = Sem(ctx, nc, "s_pc")
        s_pc0 = Sem(ctx, nc, "s_pc0")
        s_pe = Sem(ctx, nc, "s_pe")
        s_v = Sem(ctx, nc, "s_v")
        s_s = Sem(ctx, nc, "s_s")

        nc.const_aps.register(F32, 1e-5, cst[:, 0:1])
        nc.const_aps.register(F32, float(np.log(0.125)), cst[:, 1:2])
        nc.const_aps.register(F32, 1.0, cst[:, 2:3])

        block = ctx.enter_context(nc.Block())
        em = Emitter(block)

        def dma(queue, sem, out, in_, waits=()):
            sem.c += 16
            val = sem.c

            def _(h, out=out, in_=in_, waits=tuple(waits), val=val):
                for ws, wv in waits:
                    h.wait_ge(ws.h, wv)
                h.dma_start(out=out, in_=in_).then_inc(sem.h, 16)

            em.add(queue, _)
            return val

        def _mk(sem, eng):
            def emitter(fn, waits=(), inc=True, drain=False):
                val = None
                if inc:
                    sem.c += 1
                    val = sem.c

                def _(h, fn=fn, waits=tuple(waits), val=val, drain=drain):
                    for ws, wv in waits:
                        h.wait_ge(ws.h, wv)
                    if drain:
                        h.drain()
                    i = fn(h)
                    if val is not None:
                        i.then_inc(sem.h, 1)

                em.add(eng, _)
                return val

            return emitter

        pe = _mk(s_pe, "tensor")
        vec = _mk(s_v, "vector")
        sca = _mk(s_s, "scalar")


        def sccol(h_i):
            return (h_i % 2) * 512 + ((h_i % 8) // 2) * 128
        A_QT, A_KT, A_KTOK, A_V = 0, 2048, 4096, 6144
        A_SC, A_RET, A_QP, A_KP = 8192, 10240, 12288, 13312

        dma("sync", s_c, idn_s[:], idn_d)
        dma("sync", s_c, ones1[:], ones1_d)
        dma("sync", s_c, onesb[:], onesb_d)
        dma("sync", s_c, cst[:], cst_d.broadcast_to([128, 8]))
        dma("sync", s_c, etri_s[:], etri_d)
        dma("sync", s_c, tri_s[:], tri_d)
        dma("sync", s_c, erow_s[:], erow_d)
        dma("sync", s_c, idnb_s[:], idnb_d)
        dma("sync", s_c, expc[:], ecol_d)
        dma("sync", s_c, lgb[:], gam_d.broadcast_to([128, H]))
        dma("sync", s_c, bcb[:], bias_d.broadcast_to([128, E]))
        dma("sync", s_c, sel_s[:], sel_d.rearrange("j e p -> j (e p)"))
        gw_val = dma("gpsimd", s_g, gw_s[:].rearrange("p (k e) -> p k e", e=E), gw_d.rearrange("k p e -> p k e"))
        wvb_val = dma("gpsimd", s_pc0, wvb_d, wvk_d)
        wob_val = dma("gpsimd", s_pc0, wob_d, wo_d)
        wvb_val = wob_val = s_pc0.c
        cst_loaded = s_c.c
        res_ready = dma("sync", s_x, res[:].rearrange("p (k t) -> p k t", t=T),
                        xT[0].rearrange("(k p) t -> p k t", p=128))

        prev = [(s_c, cst_loaded)]

        def chain(emitfn, fn, drain=True):
            v = emitfn(fn, waits=list(prev), drain=drain)
            prev[:] = [(emitfn.__sem__, v)]
            return v

        pe.__sem__ = s_pe
        vec.__sem__ = s_v
        sca.__sem__ = s_s

        chain(vec, lambda h: h.tensor_scalar(
            out=lgb[:], in0=lgb[:], scalar1=1e-30, scalar2=None, op0=ALU.max))
        chain(sca, lambda h: h.activation(out=lgb[:], in_=lgb[:], func=AF.Ln))
        for h_i in range(H):
            pr, pc = (h_i % 2) * 64, (h_i // 2)
            chain(vec, lambda h, h_i=h_i: h.tensor_scalar(
                out=tf32[:], in0=etri_s[:], scalar1=lgb[:, h_i:h_i + 1],
                scalar2=None, op0=ALU.mult))
            chain(sca, lambda h: h.activation(out=tf32[:], in_=tf32[:], func=AF.Exp))
            dcol = (h_i // 8) * 1024 + (h_i % 2) * 512 + ((h_i % 8) // 2) * 128
            chain(vec, lambda h, dcol=dcol: h.tensor_tensor(
                out=dct[:, dcol:dcol + 128], in0=tf32[:], in1=tri_s[:],
                op=ALU.mult))
            chain(vec, lambda h, h_i=h_i, pr=pr: h.tensor_scalar(
                out=tf32[pr:pr + 64, :], in0=erow_s[pr:pr + 64, :],
                scalar1=lgb[pr:pr + 64, h_i:h_i + 1], scalar2=None, op0=ALU.mult))
            chain(sca, lambda h, pr=pr, pc=pc: h.activation(
                out=grw[pr:pr + 64, pc * 128:pc * 128 + 128],
                in_=tf32[pr:pr + 64, :], func=AF.Exp, bias=float(np.log(0.125))))
            chain(vec, lambda h, h_i=h_i: h.tensor_scalar(
                out=tf32[:, 0:64], in0=expc[:].broadcast_to([128, 64]),
                scalar1=lgb[:, h_i:h_i + 1], scalar2=None, op0=ALU.mult))
            chain(sca, lambda h, h_i=h_i: h.activation(
                out=kdc[:, h_i * 64:(h_i + 1) * 64], in_=tf32[:, 0:64], func=AF.Exp))
            chain(sca, lambda h, h_i=h_i, pr=pr, pc=pc: h.activation(
                out=gcs[pr:pr + 64, pc * 64:pc * 64 + 64],
                in_=lgb[pr:pr + 64, h_i:h_i + 1].broadcast_to([64, 64]),
                func=AF.Exp, scale=128.0))
        tables_done = list(prev)

        wslot = {"i": 0, "busy": [None, None], "sems": [s_w0, s_w1],
                 "q": ["gpsimd", "sync"]}

        def wstream(in_ap, ncols, extra=(), cast=False):
            sl = wslot["i"] % 2
            wslot["i"] += 1
            waits = list(extra)
            if wslot["busy"][sl] is not None:
                waits.append((s_pe, wslot["busy"][sl]))
            dst = wst[:, sl * 2048: sl * 2048 + ncols]
            if len(in_ap.shape) > 2:
                tail = list(in_ap.shape[1:])
                tail = tail[1:]
                d2 = dst
                if len(in_ap.shape) == 4:
                    d2 = dst.rearrange("p (a b c) -> p a b c",
                                       b=in_ap.shape[2], c=in_ap.shape[3])
                elif len(in_ap.shape) == 3:
                    d2 = dst.rearrange("p (a b) -> p a b", b=in_ap.shape[2])
                dst = d2
            sem = wslot["sems"][sl]
            q = "gpsimd" if cast else wslot["q"][sl]
            val = dma(q, sem, dst, in_ap, waits=waits)
            return sl, sl * 2048, (sem, val)

        def wslot_used(sl, pe_val):
            wslot["busy"][sl] = pe_val

        ln_cell = {"stats": None}

        def emit_ln_cast(k, t0, tn, waits=()):
            return vec(lambda h, k=k: h.tensor_copy(
                lnT[:, k * T + t0: k * T + t0 + tn],
                res[:, k * T + t0: k * T + t0 + tn]), waits=list(waits))

        def emit_ln(t0, tn, first_v_waits=(), use23=False, pre_casts=None,
                    sbf_free=None, v_fence=True, pab_extra=(), v_fence_val=None):
            SA = P2 if use23 else P0
            SB = P3 if use23 else P1
            abm = hbuf[:, 15360:15360 + tn]
            abm2 = hbuf[:, 14336:14336 + tn]
            """LN tokens [t0, t0+tn) of res -> lnT (bf16). tn in {256, 1024}."""
            ns = max(tn // 512, 1)
            nw = min(tn, 512)
            fence_v, fence_s = s_v.c, s_s.c
            prev_stats = ln_cell.get("stats")
            per_k = (isinstance(first_v_waits, list) and len(first_v_waits) == KC
                     and first_v_waits and isinstance(first_v_waits[0], list))
            cast_v, sq_s, st_pe = [], [], []
            for k in range(KC):
                if pre_casts is not None:
                    cv = pre_casts[k]
                else:
                    if per_k:
                        w = list(first_v_waits[k])
                    else:
                        w = list(first_v_waits) if k == 0 else []
                    cv = emit_ln_cast(k, t0, tn, w)
                cast_v.append(cv)
                par = k % 2
                sw = [(s_v, cv)]
                if k >= 2:
                    sw.append((s_pe, st_pe[k - 2]))
                else:
                    sw.append((s_v, fence_v if sbf_free is None else sbf_free))
                    if sbf_free is not None and prev_stats is not None:
                        sw.append((s_pe, prev_stats))
                sv = sca(lambda h, k=k, par=par: h.activation(
                    out=sbf[:, par * 1024: par * 1024 + tn],
                    in_=lnT[:, k * T + t0: k * T + t0 + tn], func=AF.Square),
                    waits=sw)
                sq_s.append(sv)
                pw = [(s_v, cv), (s_s, sv)]
                if k == 0:
                    pw += [(s_s, fence_s)]
                    if v_fence:
                        pw.append((s_v, fence_v if v_fence_val is None
                                   else v_fence_val))

                def mmfn(h, k=k, par=par):
                    last = None
                    for n in range(ns):
                        last = h.matmul(
                            SA[0:1, n * nw:(n + 1) * nw], onesb[:],
                            lnT[:, k * T + t0 + n * nw: k * T + t0 + (n + 1) * nw],
                            start=(k == 0), stop=(k == KC - 1))
                    for n in range(ns):
                        last = h.matmul(
                            SB[0:1, n * nw:(n + 1) * nw], onesb[:],
                            sbf[:, par * 1024 + n * nw: par * 1024 + (n + 1) * nw],
                            start=(k == 0), stop=(k == KC - 1))
                    return last

                pv = pe(mmfn, waits=pw)
                st_pe.append(pv)
            ln_cell["stats"] = st_pe[-1]
            sA = sca(lambda h: h.activation(out=rowb[0:1, 0:tn], in_=SA[0:1, 0:tn],
                                            func=AF.Square),
                     waits=[(s_pe, st_pe[-1])])
            v3 = vec(lambda h: h.tensor_tensor(
                out=SB[0:1, 0:tn], in0=SB[0:1, 0:tn], in1=rowb[0:1, 0:tn],
                op=ALU.subtract), waits=[(s_s, sA), (s_pe, st_pe[-1])], drain=True)
            sB = sca(lambda h: h.activation(out=rowb[0:1, 0:tn], in_=SB[0:1, 0:tn],
                                            func=AF.Ln, bias=1e-5),
                     waits=[(s_v, v3)], drain=True)
            pbc = ln_cell.get("bc")
            sC = sca(lambda h: h.activation(out=rowb[0:1, 0:tn],
                                            in_=rowb[0:1, 0:tn],
                                            func=AF.Exp, scale=-0.5), drain=True,
                     waits=[(s_c, pbc[0])] if pbc else [])
            vM = vec(lambda h: h.tensor_copy(rowb[0:1, 1024:1024 + tn],
                                             SA[0:1, 0:tn]),
                     waits=[(s_pe, st_pe[-1])]
                     + ([(s_c, pbc[1])] if pbc else []), drain=True)
            bw = [(s_v, ln_cell["app"])] if ln_cell.get("app") else []
            d1 = dma("gpsimd", s_c, lnrow_d[0:1, 0:tn], rowb[0:1, 0:tn],
                     waits=[(s_s, sC)] + ([(s_c, pbc[2])] if pbc else []))
            d2 = dma("gpsimd", s_c, lnrow_d[0:1, 1024:1024 + tn],
                     rowb[0:1, 1024:1024 + tn],
                     waits=[(s_v, vM)] + ([(s_c, pbc[3])] if pbc else []))
            bA = dma("gpsimd", s_c, abm,
                     lnrow_d[0:1, 0:tn].broadcast_to([128, tn]),
                     waits=[(s_c, d1)] + bw)
            bB = dma("gpsimd", s_c, abm2,
                     lnrow_d[0:1, 1024:1024 + tn].broadcast_to([128, tn]),
                     waits=[(s_c, d2)])
            ln_cell["bc"] = (d1, d2, bA, bB)
            app_v = []
            for k in range(KC):
                vec(lambda h, k=k: h.tensor_tensor(
                    out=lnT[:, k * T + t0:k * T + t0 + tn],
                    in0=lnT[:, k * T + t0:k * T + t0 + tn],
                    in1=abm2, op=ALU.subtract),
                    waits=[(s_c, bA), (s_c, bB)], drain=True, inc=False)
                v5 = vec(lambda h, k=k: h.tensor_tensor(
                    out=lnT[:, k * T + t0:k * T + t0 + tn],
                    in0=lnT[:, k * T + t0:k * T + t0 + tn],
                    in1=abm, op=ALU.mult), drain=True)
                app_v.append(v5)
            ln_cell["app"] = app_v[-1]
            return app_v

        def emit_gate_z(hf, ln_done, zP=None, extra_pe_waits=(),
                        extra_s_waits=(), extra_v_waits=()):
            zP = P0 if zP is None else zP
            t0 = hf * HALF
            pz = pe(lambda h: [h.matmul(
                zP[0:8, n * 512:(n + 1) * 512], gw_s[:, k * E:(k + 1) * E],
                lnT[:, k * T + t0 + n * 512: k * T + t0 + (n + 1) * 512],
                start=(k == 0), stop=(k == KC - 1))
                for k in range(KC) for n in range(2)][-1],
                waits=[(s_v, max(ln_done)), (s_g, gw_val)] + list(extra_pe_waits))
            sE = sca(lambda h: h.activation(out=rowb[0:8, 0:1024],
                                            in_=zP[0:8, 0:1024],
                                            func=AF.Exp, scale=-1.0),
                     waits=[(s_pe, pz)] + list(extra_s_waits))
            vec(lambda h: h.tensor_scalar(out=rowb[0:8, 0:1024],
                                          in0=rowb[0:8, 0:1024], scalar1=1.0,
                                          scalar2=None, op0=ALU.add),
                waits=[(s_s, sE)] + list(extra_v_waits), drain=True, inc=False)
            v2 = vec(lambda h: h.reciprocal(out=rowb[0:8, 1024:2048],
                                            in_=rowb[0:8, 0:1024]),
                     drain=True)
            return sE, v2

        def gate_tr1(hf, v2, extra=()):
            pt = pe(lambda h: [h.transpose(
                P3[:, t * 8:(t + 1) * 8],
                rowb[0:8, 1024 + t * 128: 1024 + (t + 1) * 128],
                idn_s[0:8, 0:8]) for t in range(8)][-1],
                    waits=[(s_v, v2)] + list(extra))
            v3 = vec(lambda h: h.tensor_copy(gsc[:, 0:64], P3[:, 0:64]),
                     waits=[(s_pe, pt)])

            def r3(a):
                return a.rearrange("p (t e) -> p t e", e=E)

            vec(lambda h: h.tensor_tensor(
                out=r3(gsc[:, 64:128]), in0=r3(gsc[:, 0:64]),
                in1=bcb[:, None, :].broadcast_to([128, 8, E]),
                op=ALU.add), drain=True, inc=False)
            vec(lambda h: h.tensor_reduce(
                out=gmx[:, 0:8], in_=r3(gsc[:, 64:128]), axis=AX.X, op=ALU.max),
                drain=True, inc=False)
            vec(lambda h: h.tensor_tensor(
                out=r3(gsc[:, 128:192]), in0=r3(gsc[:, 64:128]),
                in1=gmx[:, 0:8][:, :, None].broadcast_to([128, 8, E]),
                op=ALU.is_ge), drain=True, inc=False)
            vec(lambda h: h.scalar_tensor_tensor(
                out=r3(gsc[:, 192:256]), in0=r3(gsc[:, 128:192]), scalar=-1e30,
                in1=r3(gsc[:, 64:128]), op0=ALU.mult, op1=ALU.add),
                drain=True, inc=False)
            vec(lambda h: h.tensor_reduce(
                out=gmx[:, 8:16], in_=r3(gsc[:, 192:256]), axis=AX.X, op=ALU.max),
                drain=True, inc=False)
            vec(lambda h: h.tensor_tensor(
                out=r3(gsc[:, 128:192]), in0=r3(gsc[:, 64:128]),
                in1=gmx[:, 8:16][:, :, None].broadcast_to([128, 8, E]),
                op=ALU.is_ge), drain=True, inc=False)
            vA = vec(lambda h: h.scalar_tensor_tensor(
                out=gsc[:, 192:256], in0=gsc[:, 0:64], scalar=SQ5,
                in1=gsc[:, 128:192], op0=ALU.mult, op1=ALU.mult), drain=True)
            return vA, pt

        def gate_tr2(hf, vA):
            t0 = hf * HALF
            pt2 = pe(lambda h: [h.transpose(P3[0:8, t * 128:(t + 1) * 128],
                                            gsc[:, 192 + t * 8:192 + (t + 1) * 8],
                                            idn_s[:, :]) for t in range(8)][-1],
                     waits=[(s_v, vA)])
            vm = vec(lambda h, t0=t0: h.tensor_copy(mT[0:8, t0:t0 + HALF],
                                                    P3[0:8, 0:1024]),
                     waits=[(s_pe, pt2)])
            return vm

        def emit_ffn(first_v_waits=(), nexp=E, phase=3, tail_hook=None,
                     pre_casts=None, sbf_free=None, bf16_w=False,
                     first_w_extra=()):
            pc0 = pre_casts[0] if pre_casts else None
            pc1 = pre_casts[1] if pre_casts else None
            subl_fence = s_v.c
            wu_src0 = wub_d if bf16_w else wu_d
            pre_w = {}
            if phase >= 2:
                for m in (0, 2):
                    wex = first_w_extra if m == 0 else ()
                    pre_w[m] = wstream(
                        wu_src0[0, m:m + 2].rearrange("m p k c -> p m k c"),
                        2048, extra=wex, cast=not bf16_w)
            ln0 = emit_ln(0, HALF, first_v_waits=first_v_waits, pre_casts=pc0,
                          sbf_free=sbf_free)
            ln1 = emit_ln(HALF, HALF, use23=True, pre_casts=pc1,
                          pab_extra=[(s_v, subl_fence)], sbf_free=sbf_free,
                          v_fence_val=subl_fence)
            if phase == 0:
                return max(max(ln0), max(ln1))
            sE0, v20 = emit_gate_z(0, ln0 + [ln1[0]])
            pz1 = pe(lambda h: [h.matmul(
                P2[0:8, n * 512:(n + 1) * 512], gw_s[:, k * E:(k + 1) * E],
                lnT[:, k * T + HALF + n * 512: k * T + HALF + (n + 1) * 512],
                start=(k == 0), stop=(k == KC - 1))
                for k in range(KC) for n in range(2)][-1],
                waits=[(s_v, max(ln1)), (s_g, gw_val)])
            sE1 = sca(lambda h: h.activation(out=rowb[0:8, 0:1024],
                                             in_=P2[0:8, 0:1024],
                                             func=AF.Exp, scale=-1.0),
                      waits=[(s_pe, pz1), (s_v, v20)])
            gate_state = {}

            def g_tr1_0():
                vA0, pt0 = gate_tr1(0, v20, extra=[(s_v, max(ln1))])
                gate_state["vA0"] = vA0
                gate_state["pt0"] = pt0

            def g_tr2_0():
                gate_state["g0"] = gate_tr2(0, gate_state["vA0"])

            def g_sig1():
                vec(lambda h: h.tensor_scalar(out=rowb[0:8, 0:1024],
                                              in0=rowb[0:8, 0:1024], scalar1=1.0,
                                              scalar2=None, op0=ALU.add),
                    waits=[(s_s, sE1), (s_pe, gate_state["pt0"])],
                    drain=True, inc=False)
                gate_state["v21"] = vec(lambda h: h.reciprocal(
                    out=rowb[0:8, 1024:2048], in_=rowb[0:8, 0:1024]),
                    drain=True)

            def g_tr1_1():
                vA1, _pt1 = gate_tr1(1, gate_state["v21"],
                                     extra=[(s_v, gate_state["g0"])])
                gate_state["vA1"] = vA1

            def g_tr2_1():
                gate_state["g1"] = gate_tr2(1, gate_state["vA1"])

            gate_steps = {1: g_tr1_0, 4: g_tr2_0, 5: g_sig1,
                          7: g_tr1_1, 10: g_tr2_1}
            ln_done = max(max(ln0), max(ln1))
            if phase == 1:
                for mm in sorted(gate_steps):
                    gate_steps[mm]()
                return max(gate_state["g0"], gate_state["g1"])

            up_free = {}
            mbc_free = None
            dn_free = {}
            last_vd = None
            for e in range(nexp):
                prev_down_pe = s_pe.c
                wd_val = None
                for hf in range(2):
                    t0 = hf * HALF
                    sq_done = []
                    wu_cols = {}
                    for m in range(FC):
                        if m % 2 == 0:
                            wu_src = wub_d if bf16_w else wu_d
                            wex = (first_w_extra if (e == 0 and hf == 0 and m == 0)
                                   else ())
                            if e == 0 and hf == 0 and m in pre_w:
                                sl, cols, wuv = pre_w.pop(m)
                            else:
                                sl, cols, wuv = wstream(
                                    wu_src[e, m:m + 2].rearrange(
                                        "m p k c -> p m k c"),
                                    2048, extra=wex, cast=not bf16_w)
                            wu_cols = {"sl": sl, "cols": cols, "val": wuv}
                            if hf == 0 and m in (6, 8, 10, 12):
                                q = (m - 6) // 2
                                wdw = [(s_pe, prev_down_pe)]
                                wd_src = wdb_d if bf16_w else wd_d
                                wd_val = dma(
                                    "gpsimd", s_wd,
                                    wdbuf[:, q * 4 * D:(q + 1) * 4 * D].rearrange(
                                        "p (k c) -> p k c", c=D),
                                    wd_src[e, q * 4:(q + 1) * 4].rearrange(
                                        "k p c -> p k c"), waits=wdw)
                        par = m % 2
                        PU = P0 if par == 0 else P1
                        base = wu_cols["cols"] + (m % 2) * 1024
                        mmw = [wu_cols["val"], (s_v, ln_done)]
                        if e == 0 and hf == 0 and m == 0:
                            mmw.append((s_s, sE0))
                        if par in up_free:
                            mmw.append((s_s, up_free[par]))
                        pu = pe(lambda h, PU=PU, base=base, t0=t0: [h.matmul(
                            PU[:, n * 512:(n + 1) * 512],
                            wst[:, base + k * 128: base + (k + 1) * 128],
                            lnT[:, k * T + t0 + n * 512: k * T + t0 + (n + 1) * 512],
                            start=(k == 0), stop=(k == KC - 1))
                            for k in range(KC) for n in range(2)][-1],
                            waits=mmw)
                        if m % 2 == 1:
                            wslot_used(wu_cols["sl"], pu)
                        r1c = (m % 3) * 1024
                        srw = [(s_pe, pu)]
                        if len(sq_done) >= 3:
                            srw.append((s_v, sq_done[-3]))
                        sr = sca(lambda h, PU=PU, r1c=r1c: h.activation(
                            out=sbf[:, r1c:r1c + 1024], in_=PU[:, 0:1024],
                            func=AF.Relu), waits=srw)
                        sq = vec(lambda h, m=m, r1c=r1c: h.tensor_tensor(
                            out=hbuf[:, m * HALF:(m + 1) * HALF],
                            in0=sbf[:, r1c:r1c + 1024],
                            in1=sbf[:, r1c:r1c + 1024], op=ALU.mult),
                            waits=[(s_s, sr)])
                        up_free[par] = sr
                        sq_done.append(sq)
                        if e == 0 and hf == 0 and m in gate_steps:
                            gate_steps[m]()
                    if phase == 2:
                        last_vd = vmb if last_vd is None else last_vd
                        continue
                    gv = gate_state["g0"] if hf == 0 else gate_state["g1"]
                    pmw = [(s_v, gv)]
                    if 1 in up_free:
                        pmw.append((s_s, up_free[1]))
                    if mbc_free is not None:
                        pmw.append((s_v, mbc_free))
                    pm = pe(lambda h, e=e, t0=t0: [h.matmul(
                        P1[:, n * 512:(n + 1) * 512],
                        sel_s[:, e * 128:(e + 1) * 128],
                        mT[0:8, t0 + n * 512: t0 + (n + 1) * 512],
                        start=True, stop=True) for n in range(2)][-1],
                        waits=pmw)
                    smb = sca(lambda h: h.activation(out=mbc[:], in_=P1[:, 0:1024],
                                                     func=AF.Square),
                              waits=[(s_pe, pm)])
                    for md in range(KC):
                        par = md % 2
                        PD = P2 if par == 0 else P3
                        mmw = [(s_v, sq_done[11] if md == 0 else sq_done[-1])]
                        if md < 2:
                            mmw += [(s_wd, wd_val), (s_v, gv)]
                        if par in dn_free:
                            mmw.append((s_v, dn_free[par]))
                        ksplit = 12 if md == 0 else FC
                        pd = pe(lambda h, md=md, PD=PD, t0=t0, ks=ksplit: [h.matmul(
                            PD[:, n * 512:(n + 1) * 512],
                            wdbuf[:, k * D + md * 128: k * D + md * 128 + 128],
                            hbuf[:, k * HALF + n * 512: k * HALF + (n + 1) * 512],
                            start=(k == 0), stop=(k == FC - 1))
                            for k in range(ks) for n in range(2)][-1],
                            waits=mmw)
                        if md == 0:
                            pd = pe(lambda h, md=md, PD=PD, t0=t0: [h.matmul(
                                PD[:, n * 512:(n + 1) * 512],
                                wdbuf[:, k * D + md * 128: k * D + md * 128 + 128],
                                hbuf[:, k * HALF + n * 512: k * HALF + (n + 1) * 512],
                                start=(k == 0), stop=(k == FC - 1))
                                for k in range(12, FC) for n in range(2)][-1],
                                waits=[(s_v, sq_done[-1])])
                        vec(lambda h, PD=PD: h.tensor_tensor(
                            out=PD[:, 0:1024], in0=PD[:, 0:1024], in1=mbc[:],
                            op=ALU.mult), waits=[(s_pe, pd), (s_s, smb)],
                            inc=False)
                        vd = vec(lambda h, md=md, PD=PD, t0=t0: h.scalar_tensor_tensor(
                            out=res[:, md * T + t0: md * T + t0 + 1024],
                            in0=PD[:, 0:1024], scalar=1.0,
                            in1=res[:, md * T + t0: md * T + t0 + 1024],
                            op0=ALU.mult, op1=ALU.add), drain=True,
                            waits=[])
                        dn_free[par] = vd
                        last_vd = vd
                        mbc_free = vd
                        if tail_hook is not None and e == nexp - 1:
                            tail_hook(hf, md, vd)
                    emit_ffn.sbf_last = sq_done[-1]
            return last_vd

        def emit_attn(first_v_waits=(), phase=4, ngrp=NGRP,
                      tail_hook=None, pre_casts=None, sbf_free=None,
                      bf16_w=False, group_hook=None, first_w_extra=()):
            wq_src = wqb_d if bf16_w else wq_d
            wk_src = wkb_d if bf16_w else wk_d
            wq_pv, wk_pv = [], []
            for q in range(4):
                gw_ = ([(s_pe, s_pe.c)] + list(first_w_extra)) if q == 0 else []
                wq_pv.append(dma(
                    "sync" if bf16_w else "gpsimd", s_wd,
                    wdbuf[:, q * 2048:(q + 1) * 2048].rearrange(
                        "p (m c) -> p m c", c=1024),
                    wq_src[2 * q:2 * q + 2].rearrange("m p c -> p m c"),
                    waits=gw_))
            for q in range(4):
                gw_ = ([(s_pe, s_pe.c)] + list(first_w_extra)) if q == 0 else []
                wk_pv.append(dma(
                    "gpsimd", s_wk,
                    wdbuf[:, 8192 + q * 2048:8192 + (q + 1) * 2048].rearrange(
                        "p (m c) -> p m c", c=1024),
                    wk_src[2 * q:2 * q + 2].rearrange("m p c -> p m c"),
                    waits=gw_))
            wkc_val = wk_pv[-1]
            vec(lambda h: h.memset(S_all[:], 0.0), waits=[(s_v, s_v.c)], inc=False)
            sbf_ready = vec(lambda h: h.memset(S_bf[:], 0.0))
            slot_free = {}
            SLOTS = [(P0, 0), (P0, 512), (P1, 0), (P1, 512)]
            sl_i = {"i": 0}

            def proj_slot():
                i = sl_i["i"] % 4
                sl_i["i"] += 1
                return i, SLOTS[i]

            ret_done = None
            pending = []
            scev_cell = {"v": None}

            def emit_oproj(t0, ret_gate, hook=None):
                rd = None
                for m0 in range(0, KC, 2):
                    sl, cols, wov = wstream(
                        wob_d[m0:m0 + 2].rearrange("m p c -> p m c"), 2048,
                        extra=[(s_pc0, wob_val)] if m0 == 0 else ())
                    last_po = None
                    for mi in range(2):
                        m = m0 + mi
                        si, (PT, off) = proj_slot()
                        mmw = [wov, (s_v, ret_gate)]
                        if si in slot_free:
                            mmw.append((s_v, slot_free[si]))
                        po = pe(lambda h, cols=cols, mi=mi, PT=PT, off=off: [h.matmul(
                            PT[:, off:off + GRP],
                            wst[:, cols + mi * 1024 + k * 128:
                                cols + mi * 1024 + (k + 1) * 128],
                            hbuf[:, A_RET + k * GRP: A_RET + (k + 1) * GRP],
                            start=(k == 0), stop=(k == KC - 1))
                            for k in range(KC)][-1], waits=mmw)
                        last_po = po
                        vo = vec(lambda h, PT=PT, off=off, m=m, t0=t0:
                                 h.scalar_tensor_tensor(
                                     out=res[:, m * T + t0: m * T + t0 + GRP],
                                     in0=PT[:, off:off + GRP], scalar=1.0,
                                     in1=res[:, m * T + t0: m * T + t0 + GRP],
                                     op0=ALU.mult, op1=ALU.add), waits=[(s_pe, po)])
                        slot_free[si] = vo
                        rd = vo
                        if hook is not None:
                            hook(m, vo)
                    wslot_used(sl, last_po)
                return rd

            pr_prev, pst_prev = s_pe.c, s_pe.c
            pc0 = pre_casts[0] if pre_casts else None
            pc1 = pre_casts[1] if pre_casts else None
            attn_fence = s_v.c
            lnA = emit_ln(0, HALF, first_v_waits=first_v_waits, pre_casts=pc0,
                          sbf_free=sbf_free, v_fence=False)
            lnB = emit_ln(HALF, HALF, use23=True, pre_casts=pc1,
                          sbf_free=sbf_free,
                          pab_extra=[(s_v, attn_fence)],
                          v_fence_val=attn_fence)
            ln_done = max(max(lnA), max(lnB))
            for g in range(ngrp):
                t0 = g * GRP
                if group_hook is not None:
                    group_hook(g)
                if phase == 0:
                    ret_done = ln_done
                    continue
                if pending:
                    pg, pt0, prg = pending.pop(0)
                    hook = None
                    if tail_hook is not None and pg in (3, ngrp - 1):
                        hook = (lambda m, vo, pg=pg: tail_hook(
                            0 if pg == 3 else 1, m, vo))
                    ret_done = emit_oproj(pt0, prg, hook=hook)
                qkT_ev = []
                for m in range(KC):
                    for coff, dst in ((0, A_QT), (8192, A_KT)):
                        si, (PT, off) = proj_slot()
                        wsem = (s_wd, wq_pv[m // 2]) if coff == 0 \
                            else (s_wk, wk_pv[m // 2])
                        mmw = [wsem, (s_v, ln_done)]
                        if si in slot_free:
                            mmw.append((s_v, slot_free[si]))
                        pq = pe(lambda h, coff=coff, m=m, PT=PT, off=off,
                                t0=t0: [h.matmul(
                                    PT[:, off:off + GRP],
                                    wdbuf[:, coff + m * 1024 + k * 128:
                                          coff + m * 1024 + (k + 1) * 128],
                                    lnT[:, k * T + t0: k * T + t0 + GRP],
                                    start=(k == 0), stop=(k == KC - 1))
                                    for k in range(KC)][-1], waits=mmw)
                        vq = vec(lambda h, PT=PT, off=off, dst=dst, m=m:
                                 h.tensor_copy(
                                     hbuf[:, dst + m * GRP: dst + (m + 1) * GRP],
                                     PT[:, off:off + GRP]), waits=[(s_pe, pq)])
                        slot_free[si] = vq
                        qkT_ev.append(vq)
                qkT_done = max(qkT_ev)
                tokev = []
                for c in range(2):
                    PT2 = P2 if c == 0 else P3
                    tw = ([(s_v, qkT_ev[-1]),
                           (s_v, s_v.c if g == 0 else sbf_ready)]
                          if c == 0 else [])
                    ptr = pe(lambda h, c=c, PT2=PT2: [h.matmul(
                        PT2[:, m * 128:(m + 1) * 128],
                        hbuf[:, A_KT + m * GRP + c * 128:
                             A_KT + m * GRP + c * 128 + 128],
                        idnb_s[:], start=True, stop=True)
                        for m in range(KC)][-1], waits=tw)
                    vv = vec(lambda h, c=c, PT2=PT2: h.tensor_copy(
                        hbuf[:, A_KTOK + c * D: A_KTOK + (c + 1) * D],
                        PT2[:, 0:1024]), waits=[(s_pe, ptr)])
                    tokev.append(vv)
                mm_last = None
                for k0 in range(0, KC, 2):
                    sl, cols, wkv = wstream(
                        wvb_d[k0:k0 + 2].rearrange("k p c -> p k c"), 2048,
                        extra=[(s_pc0, wvb_val)] if (g == 0 and k0 == 0) else ())
                    last_pv = None
                    for ki in range(2):
                        k = k0 + ki
                        mmw = [wkv, (s_v, ln_done)]
                        if k == 0:
                            mmw.append((s_v, max(tokev)))
                        pv = pe(lambda h, k=k, cols=cols, ki=ki, t0=t0: [h.matmul(
                            (P2 if t == 0 else P3)[:, n * 512:(n + 1) * 512],
                            lnT[:, k * T + t0 + t * 128:
                                k * T + t0 + t * 128 + 128],
                            wst[:, cols + ki * 1024 + n * 512:
                                cols + ki * 1024 + (n + 1) * 512],
                            start=(k == 0), stop=(k == KC - 1))
                            for t in range(2) for n in range(2)][-1],
                            waits=mmw)
                        last_pv = pv
                        mm_last = pv
                    wslot_used(sl, last_pv)
                for t in range(2):
                    vv = vec(lambda h, t=t: h.tensor_copy(
                        hbuf[:, A_V + t * D: A_V + (t + 1) * D],
                        (P2 if t == 0 else P3)[:, 0:1024]),
                        waits=[(s_pe, mm_last)])
                    tokev.append(vv)
                tok_done = max(tokev)
                if phase == 2:
                    ret_done = tok_done
                    continue
                QP_OFF = (A_QP, 14336)
                KP_OFF = (A_KP, 15360)
                sc_pre = []
                for c in range(2):
                    ct = c * 128
                    w = [(s_v, qkT_done)]
                    if c == 0:
                        w += [(s_pe, pr_prev), (s_pe, pst_prev)]
                    vqp = vec(lambda h, ct=ct, c=c: h.tensor_tensor(
                        out=hbuf[:, QP_OFF[c]:QP_OFF[c] + 1024].rearrange(
                            "p (k c2) -> p k c2", c2=128),
                        in0=hbuf[:, A_QT:A_QT + 2048].rearrange(
                            "p (k c2) -> p k c2", c2=GRP)[:, :, ct:ct + 128],
                        in1=grw[:].rearrange("p (k c2) -> p k c2", c2=128),
                        op=ALU.mult), waits=w)
                    vkp = vec(lambda h, c=c: h.tensor_tensor(
                        out=hbuf[:, KP_OFF[c]:KP_OFF[c] + 1024],
                        in0=hbuf[:, A_KTOK + c * D: A_KTOK + (c + 1) * D],
                        in1=kdc[:], op=ALU.mult),
                        waits=[(s_v, tok_done)])
                    sc_pre.append((vqp, vkp))
                ret_ev = []
                sub = ATTN_SUB
                for c in range(2):
                    ct = c * 128
                    vqp, vkp = sc_pre[c]
                    if sub == 0:
                        ret_ev = [vqp, vkp]
                        continue
                    sc_ev = []
                    for hs in range(2):
                        mmw = [(s_v, qkT_done)]
                        if hs == 0 and c == 0:
                            mmw.append((s_v, tok_done))
                        if hs == 0 and scev_cell["v"] is not None:
                            mmw.append((s_v, scev_cell["v"]))
                        if sc_ev:
                            mmw.append((s_v, sc_ev[-1]))
                        ps = pe(lambda h, hs=hs, ct=ct: [h.matmul(
                            P2[:, sccol(h_i):sccol(h_i) + 128],
                            hbuf[:, A_KT + (h_i // 2) * GRP + ct:
                                 A_KT + (h_i // 2) * GRP + ct + 128][
                                (h_i % 2) * 64:(h_i % 2) * 64 + 64, :],
                            hbuf[:, A_QT + (h_i // 2) * GRP + ct:
                                 A_QT + (h_i // 2) * GRP + ct + 128][
                                (h_i % 2) * 64:(h_i % 2) * 64 + 64, :],
                            start=True, stop=True)
                            for h_i in range(hs * 8, hs * 8 + 8)][-1],
                            waits=mmw)
                        vsc = vec(lambda h, hs=hs: h.tensor_tensor(
                            out=hbuf[:, A_SC + hs * 1024: A_SC + (hs + 1) * 1024],
                            in0=P2[:, 0:1024],
                            in1=dct[:, hs * 1024:(hs + 1) * 1024], op=ALU.mult),
                            waits=[(s_pe, ps)])
                        sc_ev.append(vsc)
                    scev_cell["v"] = sc_ev[-1]
                    if sub == 1:
                        ret_ev = [max(sc_ev)]
                        continue
                    if sub >= 4:
                        pst = pe(lambda h, c=c: [h.matmul(
                            P2[(h_i % 2) * 64:(h_i % 2) * 64 + 64,
                               (h_i // 2) * 64:(h_i // 2) * 64 + 64],
                            hbuf[:, KP_OFF[c] + h_i * 64: KP_OFF[c] + h_i * 64 + 64],
                            hbuf[:, A_V + c * D + h_i * 64:
                                 A_V + c * D + h_i * 64 + 64],
                            start=True, stop=True,
                            tile_position=(0, (h_i % 2) * 64)) for h_i in range(H)][-1],
                            waits=[(s_v, max(vkp, sc_ev[-1]))])
                        pst_prev = pst
                        vec(lambda h: h.tensor_tensor(
                            out=S_all[:], in0=S_all[:], in1=gcs[:], op=ALU.mult),
                            drain=True, inc=False)
                        s_add = vec(lambda h: h.tensor_tensor(
                            out=S_all[:], in0=S_all[:], in1=P2[:, 0:512],
                            op=ALU.add), waits=[(s_pe, pst)], drain=True)
                        scev_cell["v"] = max(scev_cell["v"], s_add)
                    mmw = [(s_v, max(sc_ev[-1], vqp, sbf_ready))]
                    if c == 0:
                        mmw.append((s_v, tok_done))
                    if ret_ev:
                        mmw.append((s_v, ret_ev[-1]))
                    def retfn(h, c=c, sub=sub):
                        last = None
                        for h_i in range(H):
                            rr = (h_i % 2) * 64
                            pcol = (h_i // 2)
                            mm1 = h.matmul(
                                P3[rr:rr + 64, pcol * 128:pcol * 128 + 128],
                                hbuf[:, A_V + c * D + h_i * 64:
                                     A_V + c * D + h_i * 64 + 64],
                                hbuf[:, A_SC + (h_i // 8) * 1024 + sccol(h_i):
                                     A_SC + (h_i // 8) * 1024 + sccol(h_i) + 128],
                                start=True, stop=(sub == 2), tile_position=(0, rr))
                            if sub == 2:
                                last = mm1
                                continue
                            last = h.matmul(
                                P3[rr:rr + 64, pcol * 128:pcol * 128 + 128],
                                S_bf[rr:rr + 64, pcol * 64:pcol * 64 + 64],
                                hbuf[:, QP_OFF[c] + pcol * 128:
                                     QP_OFF[c] + pcol * 128 + 128][rr:rr + 64, :],
                                start=False, stop=True, tile_position=(rr, rr))
                        return last

                    pr = pe(retfn, waits=mmw)
                    pr_prev = pr
                    vre = vec(lambda h, ct=ct: h.tensor_copy(
                        hbuf[:, A_RET:A_RET + 2048].rearrange(
                            "p (k c2) -> p k c2", c2=GRP)[:, :, ct:ct + 128],
                        P3[:, 0:1024].rearrange("p (k c2) -> p k c2", c2=128)),
                        waits=[(s_pe, pr)])
                    ret_ev.append(vre)
                    if sub == 3:
                        continue
                    sbf_ready = vec(lambda h: h.tensor_copy(S_bf[:], S_all[:]),
                                    waits=[(s_pe, pr)], drain=True)
                if phase == 3:
                    ret_done = max(ret_ev)
                    continue
                pending.append((g, t0, max(ret_ev)))
            emit_attn.sbf_last = ln_done
            while pending:
                pg, pt0, prg = pending.pop(0)
                hook = None
                if tail_hook is not None and pg in (3, ngrp - 1):
                    hook = (lambda m, vo, pg=pg: tail_hook(0 if pg == 3 else 1,
                                                           m, vo))
                ret_done = emit_oproj(pt0, prg, hook=hook)
            return ret_done

        last_store = 0
        resv_k = [(s_x, res_ready)] * KC
        for b in range(n_batches):
            done = s_v.c
            fw = [[w] for w in resv_k]
            full = sublayers == (0, 1, 2) and FFN_PHASE == 3 and ATTN_PHASE == 4
            attn_pc = [[None] * KC, [None] * KC]
            ffn2_pc = [[None] * KC, [None] * KC]

            def ffn1_hook(hf, md, vd):
                attn_pc[hf][md] = emit_ln_cast(md, hf * HALF, HALF,
                                               waits=[(s_v, vd)])

            def attn_hook(hf, m, vo):
                ffn2_pc[hf][m] = emit_ln_cast(m, hf * HALF, HALF,
                                              waits=[(s_v, vo)])

            next_resv = [None] * KC

            def ffn2_hook(hf, md, vd, b=b):
                if hf != 1:
                    return
                vfk = vec(lambda h: h.tensor_copy(gsc[:, 8:16], gsc[:, 8:16]),
                          waits=[(s_v, vd)], drain=True)
                st = dma("sync", s_ch[md],
                         out_d[b].rearrange("(k p) t -> k p t", p=128)[md],
                         res[:, md * T:(md + 1) * T], waits=[(s_v, vfk)])
                if b + 1 < n_batches:
                    ld = dma("sync", s_ch[md], res[:, md * T:(md + 1) * T],
                             xT[b + 1].rearrange("(k p) t -> k p t", p=128)[md],
                             waits=[(s_ch[md], st)])
                    next_resv[md] = (s_ch[md], ld)

            def cast_hook(g):
                if g < E:
                    dma("gpsimd", s_pc, wub_d[g, 0:8], wu_d[g, 0:8])
                    dma("gpsimd", s_pc, wub_d[g, 8:16], wu_d[g, 8:16])
                    dma("gpsimd", s_pc, wdb_d[g, 0:8], wd_d[g, 0:8])
                    dma("gpsimd", s_pc, wdb_d[g, 8:16], wd_d[g, 8:16])
                if g == 6:
                    dma("gpsimd", s_pc, wqb_d, wq_d)
                    dma("gpsimd", s_pc, wkb_d, wk_d)

            bf16w = full and b > 0
            if 0 in sublayers:
                done = emit_ffn(first_v_waits=fw, nexp=FFN_NEXP, phase=FFN_PHASE,
                                tail_hook=ffn1_hook if full else None,
                                bf16_w=bf16w,
                                first_w_extra=[(s_pc, s_pc.c)] if bf16w else ())
                fw = []
            if 1 in sublayers:
                done = emit_attn(first_v_waits=fw, phase=ATTN_PHASE,
                                 ngrp=ATTN_NGRP,
                                 tail_hook=attn_hook if full else None,
                                 pre_casts=attn_pc if full else None,
                                 sbf_free=getattr(emit_ffn, "sbf_last", None)
                                 if full else None,
                                 bf16_w=bf16w,
                                 group_hook=cast_hook if (full and b == 0)
                                 else None,
                                 first_w_extra=[(s_pc, s_pc.c)] if bf16w else ())
                fw = []
            if 2 in sublayers:
                done = emit_ffn(first_v_waits=fw,
                                pre_casts=ffn2_pc if full else None,
                                sbf_free=getattr(emit_attn, "sbf_last", None)
                                if full else None,
                                bf16_w=(full and True),
                                first_w_extra=[(s_pc, s_pc.c)]
                                if (full and b == 0) else (),
                                tail_hook=ffn2_hook if full else None)
                fw = []
            if full:
                if b + 1 < n_batches:
                    assert all(v is not None for v in next_resv)
                    resv_k = next_resv
            else:
                vf = vec(lambda h: h.tensor_copy(gsc[:, 0:8], gsc[:, 0:8]),
                         waits=[(s_v, done)], drain=True)
                new_resv = []
                st_vals = []
                for k in range(KC):
                    st_k = dma("sync", s_ch[k],
                               out_d[b].rearrange("(k p) t -> k p t", p=128)[k],
                               res[:, k * T:(k + 1) * T], waits=[(s_v, vf)])
                    st_vals.append(st_k)
                for k in range(KC):
                    if b + 1 < n_batches:
                        ld_k = dma("sync", s_ch[k], res[:, k * T:(k + 1) * T],
                                   xT[b + 1].rearrange("(k p) t -> k p t",
                                                       p=128)[k],
                                   waits=[(s_ch[k], st_vals[k])])
                        new_resv.append((s_ch[k], ld_k))
                if new_resv:
                    resv_k = new_resv
        fin = s_x.c
        fin_ch = [s.c for s in s_ch]

        def _(h, fin=fin, fin_ch=tuple(fin_ch), sems=tuple(s.h for s in s_ch)):
            h.wait_ge(s_x.h, fin)
            for sh, fv in zip(sems, fin_ch):
                h.wait_ge(sh, fv)

        em.add("sync", _)
        em.flush()
    nc.finalize()
    return nc


def _prepack(inputs):
    x = np.asarray(inputs["L0_attn_x"], dtype=np.float32)
    wq = np.asarray(inputs["L0_attn_wq"], dtype=np.float32)
    wk = np.asarray(inputs["L0_attn_wk"], dtype=np.float32)
    wv = np.asarray(inputs["L0_attn_wv"], dtype=np.float32)
    wo = np.asarray(inputs["L0_attn_wo"], dtype=np.float32)
    gam = np.asarray(inputs["L0_attn_gammas"], dtype=np.float32)
    gw = np.asarray(inputs["L0_ffn_gate_w"], dtype=np.float32)
    wu = np.asarray(inputs["L0_ffn_w_up"], dtype=np.float32)
    wd = np.asarray(inputs["L0_ffn_w_down"], dtype=np.float32)
    eb = np.asarray(inputs["L0_ffn_expert_bias"], dtype=np.float32)

    def mmaj(w):
        return np.ascontiguousarray(
            w.reshape(KC, 128, KC, 128).transpose(0, 3, 2, 1).reshape(KC, 128, D))

    def kmaj(w):
        return np.ascontiguousarray(w.reshape(D, KC, 128).transpose(1, 2, 0))

    packs = {
        "wu_p": np.ascontiguousarray(
            wu.reshape(E, FC, 128, KC, 128).transpose(0, 1, 4, 3, 2)),
        "wd_p": np.ascontiguousarray(
            wd.reshape(E, D, FC, 128).transpose(0, 2, 3, 1)),
        "wq_m": mmaj(wq), "wk_m": mmaj(wk), "wo_m": mmaj(wo),
        "wk_k": kmaj(wk), "wv_k": kmaj(wv),
        "gw_p": np.ascontiguousarray(gw.reshape(E, KC, 128).transpose(1, 2, 0)),
        "ebias": np.ascontiguousarray(eb.reshape(1, E)),
        "gam": np.ascontiguousarray(gam.reshape(1, H)),
    }
    tau = np.arange(128, dtype=np.float32)
    consts = {
        "etri": np.maximum(tau[None, :] - tau[:, None], 0.0).astype(ml_dtypes.bfloat16),
        "tri": ((tau[None, :] >= tau[:, None]) * 0.125).astype(ml_dtypes.bfloat16),
        "erow": np.tile(tau[None, :] + 1.0, (128, 1)).astype(ml_dtypes.bfloat16),
        "ecol": np.ascontiguousarray((127.0 - tau).reshape(128, 1)),
        "idn": np.eye(128, dtype=np.float32),
        "idnb": np.eye(128, dtype=ml_dtypes.bfloat16),
        "ones1": np.ones((1, 128), dtype=np.float32),
        "onesb": np.full((128, 1), 1.0 / D, dtype=ml_dtypes.bfloat16),
        "sel_p": np.ascontiguousarray(
            np.eye(E, dtype=ml_dtypes.bfloat16)[:, :, None].repeat(128, 2)),
        "cvals": np.array([[1e-5, np.log(0.125), 1.0, 0, 0, 0, 0, 0]], np.float32),
    }
    xt = np.ascontiguousarray(x.reshape(NCORE, BPC, T, D).transpose(0, 1, 3, 2))
    return xt, packs, consts


def kernel(**inputs):
    xt, packs, consts = _prepack(inputs)
    nc = build()
    in_maps = []
    for c in range(NCORE):
        m = {"xT": xt[c]}
        m.update(packs)
        m.update(consts)
        in_maps.append(m)
    res = run_bass_kernel_spmd(nc, in_maps, core_ids=list(range(NCORE)))
    out = np.stack([r["outT"] for r in res.results])
    out = out.reshape(NCORE * BPC, D, T).transpose(0, 2, 1)
    return np.ascontiguousarray(out)


# version 29
# speedup vs baseline: 1.0099x; 1.0099x over previous
import numpy as np
import ml_dtypes
from contextlib import ExitStack

import concourse.bass as bass
import concourse.mybir as mybir
from concourse.bass_utils import run_bass_kernel_spmd

F32 = mybir.dt.float32
BF16 = mybir.dt.bfloat16
ALU = mybir.AluOpType
AF = mybir.ActivationFunctionType
AX = mybir.AxisListType

D = 1024
T = 2048
KC = 8
FC = 16
E = 8
F = 2048
H = 16
HD = 64
GRP = 256
NGRP = T // GRP
HALF = 1024
NCORE = 8
BPC = 4
SQ5 = float(np.sqrt(0.5))

ENGINES = ("sync", "gpsimd", "tensor", "vector", "scalar")


class Sem:
    def __init__(self, ctx, nc, name):
        self.h = ctx.enter_context(nc.semaphore(name))
        self.c = 0


class Emitter:

    def __init__(self, block):
        self.block = block
        self.q = {e: [] for e in ENGINES}

    def add(self, eng, fn):
        self.q[eng].append(fn)

    def flush(self):
        for eng in ENGINES:
            fns = self.q[eng]
            if not fns:
                continue

            def run(h, fns=fns):
                for f in fns:
                    f(h)

            getattr(self.block, eng)(run)
        self.q = {e: [] for e in ENGINES}


def build(n_batches=BPC, sublayers=(0, 1, 2), FFN_NEXP=E, FFN_PHASE=3,
          ATTN_PHASE=4, ATTN_NGRP=NGRP, ATTN_SUB=4):
    nc = bass.Bass("TRN2", target_bir_lowering=False)

    xT = nc.dram_tensor("xT", [BPC, D, T], F32, kind="ExternalInput").ap()
    out_d = nc.dram_tensor("outT", [BPC, D, T], F32, kind="ExternalOutput").ap()
    wu_d = nc.dram_tensor("wu_p", [E, FC, 128, KC, 128], F32, kind="ExternalInput").ap()
    wd_d = nc.dram_tensor("wd_p", [E, FC, 128, D], F32, kind="ExternalInput").ap()
    wq_d = nc.dram_tensor("wq_m", [KC, 128, D], F32, kind="ExternalInput").ap()
    wk_d = nc.dram_tensor("wk_m", [KC, 128, D], F32, kind="ExternalInput").ap()
    wo_d = nc.dram_tensor("wo_m", [KC, 128, D], F32, kind="ExternalInput").ap()
    wkk_d = nc.dram_tensor("wk_k", [KC, 128, D], F32, kind="ExternalInput").ap()
    wvk_d = nc.dram_tensor("wv_k", [KC, 128, D], F32, kind="ExternalInput").ap()
    gw_d = nc.dram_tensor("gw_p", [KC, 128, E], F32, kind="ExternalInput").ap()
    bias_d = nc.dram_tensor("ebias", [1, E], F32, kind="ExternalInput").ap()
    gam_d = nc.dram_tensor("gam", [1, H], F32, kind="ExternalInput").ap()
    etri_d = nc.dram_tensor("etri", [128, 128], BF16, kind="ExternalInput").ap()
    tri_d = nc.dram_tensor("tri", [128, 128], BF16, kind="ExternalInput").ap()
    erow_d = nc.dram_tensor("erow", [128, 128], BF16, kind="ExternalInput").ap()
    ecol_d = nc.dram_tensor("ecol", [128, 1], F32, kind="ExternalInput").ap()
    idn_d = nc.dram_tensor("idn", [128, 128], F32, kind="ExternalInput").ap()
    ones1_d = nc.dram_tensor("ones1", [1, 128], F32, kind="ExternalInput").ap()
    onesb_d = nc.dram_tensor("onesb", [128, 1], BF16, kind="ExternalInput").ap()
    idnb_d = nc.dram_tensor("idnb", [128, 128], BF16, kind="ExternalInput").ap()
    sel_d = nc.dram_tensor("sel_p", [E, E, 128], BF16, kind="ExternalInput").ap()
    cst_d = nc.dram_tensor("cvals", [1, 8], F32, kind="ExternalInput").ap()
    lnrow_d = nc.dram_tensor("lnrow", [1, 2048], BF16, kind="Internal").ap()
    wvb_d = nc.dram_tensor("wv_b16", [KC, 128, D], BF16, kind="Internal").ap()
    wob_d = nc.dram_tensor("wo_b16", [KC, 128, D], BF16, kind="Internal").ap()
    wqb_d = nc.dram_tensor("wq_b16", [KC, 128, D], BF16, kind="Internal").ap()
    wkb_d = nc.dram_tensor("wk_b16", [KC, 128, D], BF16, kind="Internal").ap()
    wub_d = nc.dram_tensor("wu_b16", [E, FC, 128, KC, 128], BF16,
                           kind="Internal").ap()
    wdb_d = nc.dram_tensor("wd_b16", [E, FC, 128, D], BF16, kind="Internal").ap()

    ctx = ExitStack()
    with ctx:
        sbt = nc.sbuf_tensor
        res = ctx.enter_context(sbt("res", [128, KC * T], F32))
        lnT = ctx.enter_context(sbt("lnT", [128, KC * T], BF16))
        hbuf = ctx.enter_context(sbt("hbuf", [128, FC * HALF], BF16))
        wdbuf = ctx.enter_context(sbt("wdbuf", [128, FC * D], BF16))
        wst = ctx.enter_context(sbt("wst", [128, 2 * 2048], BF16))
        sbf = ctx.enter_context(sbt("sbf", [128, 4096], BF16))
        gsc = ctx.enter_context(sbt("gsc", [128, 256], F32))
        rowb = ctx.enter_context(sbt("rowb", [8, 2048], F32))
        mbc = sbf[:, 3072:4096]
        mT = ctx.enter_context(sbt("mT", [8, T], BF16))
        gmx = ctx.enter_context(sbt("gmx", [128, 32], F32))
        dct = ctx.enter_context(sbt("dct", [128, H * 128], BF16))
        grw = ctx.enter_context(sbt("grw", [128, KC * 128], BF16))
        kdc = ctx.enter_context(sbt("kdc", [128, D], BF16))
        gcs = ctx.enter_context(sbt("gcs", [128, 512], BF16))
        lgb = ctx.enter_context(sbt("lgb", [128, H], F32))
        bcb = ctx.enter_context(sbt("bcb", [128, E], F32))
        expc = ctx.enter_context(sbt("expc", [128, 1], F32))
        idn_s = ctx.enter_context(sbt("idn_s", [128, 128], F32))
        idnb_s = ctx.enter_context(sbt("idnb_s", [128, 128], BF16))
        onesb = ctx.enter_context(sbt("onesbf", [128, 1], BF16))
        sel_s = ctx.enter_context(sbt("sel_s", [E, E * 128], BF16))
        gw_s = ctx.enter_context(sbt("gw_s", [128, KC * E], BF16))
        cst = ctx.enter_context(sbt("cst_s", [128, 8], F32))
        tri_s = ctx.enter_context(sbt("tri_s", [128, 128], BF16))
        etri_s = ctx.enter_context(sbt("etri_s", [128, 128], BF16))
        S_bf2 = ctx.enter_context(sbt("S_bf2", [128, 512], BF16))
        erow_s = sbf[:, 0:128]
        tf32 = gsc[:, 0:128]
        S_all = ctx.enter_context(sbt("S_all", [128, 512], F32))
        S_bf = ctx.enter_context(sbt("S_bf", [128, 512], BF16))

        P0 = ctx.enter_context(nc.psum_tensor("P0", [128, 1024], F32))
        P1 = ctx.enter_context(nc.psum_tensor("P1", [128, 1024], F32))
        P2 = ctx.enter_context(nc.psum_tensor("P2", [128, 1024], F32))
        P3 = ctx.enter_context(nc.psum_tensor("P3", [128, 1024], F32))

        s_x = Sem(ctx, nc, "s_x")
        s_c = Sem(ctx, nc, "s_c")
        s_g = Sem(ctx, nc, "s_g")
        s_wd = Sem(ctx, nc, "s_wd")
        s_wk = Sem(ctx, nc, "s_wk")
        s_w0 = Sem(ctx, nc, "s_w0")
        s_w1 = Sem(ctx, nc, "s_w1")
        s_w2 = Sem(ctx, nc, "s_w2")
        s_ch = [Sem(ctx, nc, f"s_ch{k}") for k in range(KC)]
        s_pc = Sem(ctx, nc, "s_pc")
        s_pc0 = Sem(ctx, nc, "s_pc0")
        s_pe = Sem(ctx, nc, "s_pe")
        s_v = Sem(ctx, nc, "s_v")
        s_s = Sem(ctx, nc, "s_s")

        nc.const_aps.register(F32, 1e-5, cst[:, 0:1])
        nc.const_aps.register(F32, float(np.log(0.125)), cst[:, 1:2])
        nc.const_aps.register(F32, 1.0, cst[:, 2:3])

        block = ctx.enter_context(nc.Block())
        em = Emitter(block)

        def dma(queue, sem, out, in_, waits=()):
            sem.c += 16
            val = sem.c

            def _(h, out=out, in_=in_, waits=tuple(waits), val=val):
                for ws, wv in waits:
                    h.wait_ge(ws.h, wv)
                h.dma_start(out=out, in_=in_).then_inc(sem.h, 16)

            em.add(queue, _)
            return val

        def _mk(sem, eng):
            def emitter(fn, waits=(), inc=True, drain=False):
                val = None
                if inc:
                    sem.c += 1
                    val = sem.c

                def _(h, fn=fn, waits=tuple(waits), val=val, drain=drain):
                    for ws, wv in waits:
                        h.wait_ge(ws.h, wv)
                    if drain:
                        h.drain()
                    i = fn(h)
                    if val is not None:
                        i.then_inc(sem.h, 1)

                em.add(eng, _)
                return val

            return emitter

        pe = _mk(s_pe, "tensor")
        vec = _mk(s_v, "vector")
        sca = _mk(s_s, "scalar")


        def sccol(h_i):
            return (h_i % 2) * 512 + ((h_i % 8) // 2) * 128
        A_QT, A_KT, A_KTOK, A_V = 0, 2048, 4096, 6144
        A_SC, A_RET, A_QP, A_KP = 8192, 10240, 12288, 13312

        dma("sync", s_c, idn_s[:], idn_d)
        dma("sync", s_c, onesb[:], onesb_d)
        dma("sync", s_c, cst[:], cst_d.broadcast_to([128, 8]))
        dma("sync", s_c, etri_s[:], etri_d)
        dma("sync", s_c, tri_s[:], tri_d)
        dma("sync", s_c, erow_s[:], erow_d)
        dma("sync", s_c, idnb_s[:], idnb_d)
        dma("sync", s_c, expc[:], ecol_d)
        dma("sync", s_c, lgb[:], gam_d.broadcast_to([128, H]))
        dma("sync", s_c, bcb[:], bias_d.broadcast_to([128, E]))
        dma("sync", s_c, sel_s[:], sel_d.rearrange("j e p -> j (e p)"))
        gw_val = dma("gpsimd", s_g, gw_s[:].rearrange("p (k e) -> p k e", e=E), gw_d.rearrange("k p e -> p k e"))
        wvb_val = dma("gpsimd", s_pc0, wvb_d, wvk_d)
        wob_val = dma("gpsimd", s_pc0, wob_d, wo_d)
        wvb_val = wob_val = s_pc0.c
        cst_loaded = s_c.c
        res_ready = dma("sync", s_x, res[:].rearrange("p (k t) -> p k t", t=T),
                        xT[0].rearrange("(k p) t -> p k t", p=128))

        prev = [(s_c, cst_loaded)]

        def chain(emitfn, fn, drain=True):
            v = emitfn(fn, waits=list(prev), drain=drain)
            prev[:] = [(emitfn.__sem__, v)]
            return v

        pe.__sem__ = s_pe
        vec.__sem__ = s_v
        sca.__sem__ = s_s

        chain(vec, lambda h: h.tensor_scalar(
            out=lgb[:], in0=lgb[:], scalar1=1e-30, scalar2=None, op0=ALU.max))
        chain(sca, lambda h: h.activation(out=lgb[:], in_=lgb[:], func=AF.Ln))
        for h_i in range(H):
            pr, pc = (h_i % 2) * 64, (h_i // 2)
            chain(vec, lambda h, h_i=h_i: h.tensor_scalar(
                out=tf32[:], in0=etri_s[:], scalar1=lgb[:, h_i:h_i + 1],
                scalar2=None, op0=ALU.mult))
            chain(sca, lambda h: h.activation(out=tf32[:], in_=tf32[:], func=AF.Exp))
            dcol = (h_i // 8) * 1024 + (h_i % 2) * 512 + ((h_i % 8) // 2) * 128
            chain(vec, lambda h, dcol=dcol: h.tensor_tensor(
                out=dct[:, dcol:dcol + 128], in0=tf32[:], in1=tri_s[:],
                op=ALU.mult))
            chain(vec, lambda h, h_i=h_i, pr=pr: h.tensor_scalar(
                out=tf32[pr:pr + 64, :], in0=erow_s[pr:pr + 64, :],
                scalar1=lgb[pr:pr + 64, h_i:h_i + 1], scalar2=None, op0=ALU.mult))
            chain(sca, lambda h, pr=pr, pc=pc: h.activation(
                out=grw[pr:pr + 64, pc * 128:pc * 128 + 128],
                in_=tf32[pr:pr + 64, :], func=AF.Exp, bias=float(np.log(0.125))))
            chain(vec, lambda h, h_i=h_i: h.tensor_scalar(
                out=tf32[:, 0:64], in0=expc[:].broadcast_to([128, 64]),
                scalar1=lgb[:, h_i:h_i + 1], scalar2=None, op0=ALU.mult))
            chain(sca, lambda h, h_i=h_i: h.activation(
                out=kdc[:, h_i * 64:(h_i + 1) * 64], in_=tf32[:, 0:64], func=AF.Exp))
            chain(sca, lambda h, h_i=h_i, pr=pr, pc=pc: h.activation(
                out=gcs[pr:pr + 64, pc * 64:pc * 64 + 64],
                in_=lgb[pr:pr + 64, h_i:h_i + 1].broadcast_to([64, 64]),
                func=AF.Exp, scale=128.0))
        tables_done = list(prev)

        wslot = {"i": 0, "busy": [None, None], "sems": [s_w0, s_w1],
                 "q": ["gpsimd", "sync"]}

        def wstream(in_ap, ncols, extra=(), cast=False):
            sl = wslot["i"] % 2
            wslot["i"] += 1
            waits = list(extra)
            if wslot["busy"][sl] is not None:
                waits.append((s_pe, wslot["busy"][sl]))
            dst = wst[:, sl * 2048: sl * 2048 + ncols]
            if len(in_ap.shape) > 2:
                tail = list(in_ap.shape[1:])
                tail = tail[1:]
                d2 = dst
                if len(in_ap.shape) == 4:
                    d2 = dst.rearrange("p (a b c) -> p a b c",
                                       b=in_ap.shape[2], c=in_ap.shape[3])
                elif len(in_ap.shape) == 3:
                    d2 = dst.rearrange("p (a b) -> p a b", b=in_ap.shape[2])
                dst = d2
            sem = wslot["sems"][sl]
            q = "gpsimd" if cast else wslot["q"][sl]
            val = dma(q, sem, dst, in_ap, waits=waits)
            return sl, sl * 2048, (sem, val)

        def wslot_used(sl, pe_val):
            wslot["busy"][sl] = pe_val

        ln_cell = {"stats": None}

        def emit_ln_cast(k, t0, tn, waits=()):
            return vec(lambda h, k=k: h.tensor_copy(
                lnT[:, k * T + t0: k * T + t0 + tn],
                res[:, k * T + t0: k * T + t0 + tn]), waits=list(waits))

        def emit_ln(t0, tn, first_v_waits=(), use23=False, pre_casts=None,
                    sbf_free=None, v_fence=True, pab_extra=(), v_fence_val=None):
            SA = P2 if use23 else P0
            SB = P3 if use23 else P1
            abm = hbuf[:, 15360:15360 + tn]
            abm2 = hbuf[:, 14336:14336 + tn]
            """LN tokens [t0, t0+tn) of res -> lnT (bf16). tn in {256, 1024}."""
            ns = max(tn // 512, 1)
            nw = min(tn, 512)
            fence_v, fence_s = s_v.c, s_s.c
            prev_stats = ln_cell.get("stats")
            per_k = (isinstance(first_v_waits, list) and len(first_v_waits) == KC
                     and first_v_waits and isinstance(first_v_waits[0], list))
            cast_v, sq_s, st_pe = [], [], []
            for k in range(KC):
                if pre_casts is not None:
                    cv = pre_casts[k]
                else:
                    if per_k:
                        w = list(first_v_waits[k])
                    else:
                        w = list(first_v_waits) if k == 0 else []
                    cv = emit_ln_cast(k, t0, tn, w)
                cast_v.append(cv)
                par = k % 2
                sw = [(s_v, cv)]
                if k >= 2:
                    sw.append((s_pe, st_pe[k - 2]))
                else:
                    sw.append((s_v, fence_v if sbf_free is None else sbf_free))
                    if sbf_free is not None and prev_stats is not None:
                        sw.append((s_pe, prev_stats))
                sv = sca(lambda h, k=k, par=par: h.activation(
                    out=sbf[:, par * 1024: par * 1024 + tn],
                    in_=lnT[:, k * T + t0: k * T + t0 + tn], func=AF.Square),
                    waits=sw)
                sq_s.append(sv)
                pw = [(s_v, cv), (s_s, sv)]
                if k == 0:
                    pw += [(s_s, fence_s)]
                    if v_fence:
                        pw.append((s_v, fence_v if v_fence_val is None
                                   else v_fence_val))

                def mmfn(h, k=k, par=par):
                    last = None
                    for n in range(ns):
                        last = h.matmul(
                            SA[0:1, n * nw:(n + 1) * nw], onesb[:],
                            lnT[:, k * T + t0 + n * nw: k * T + t0 + (n + 1) * nw],
                            start=(k == 0), stop=(k == KC - 1))
                    for n in range(ns):
                        last = h.matmul(
                            SB[0:1, n * nw:(n + 1) * nw], onesb[:],
                            sbf[:, par * 1024 + n * nw: par * 1024 + (n + 1) * nw],
                            start=(k == 0), stop=(k == KC - 1))
                    return last

                pv = pe(mmfn, waits=pw)
                st_pe.append(pv)
            ln_cell["stats"] = st_pe[-1]
            sA = sca(lambda h: h.activation(out=rowb[0:1, 0:tn], in_=SA[0:1, 0:tn],
                                            func=AF.Square),
                     waits=[(s_pe, st_pe[-1])])
            v3 = vec(lambda h: h.tensor_tensor(
                out=SB[0:1, 0:tn], in0=SB[0:1, 0:tn], in1=rowb[0:1, 0:tn],
                op=ALU.subtract), waits=[(s_s, sA), (s_pe, st_pe[-1])], drain=True)
            sB = sca(lambda h: h.activation(out=rowb[0:1, 0:tn], in_=SB[0:1, 0:tn],
                                            func=AF.Ln, bias=1e-5),
                     waits=[(s_v, v3)], drain=True)
            pbc = ln_cell.get("bc")
            sC = sca(lambda h: h.activation(out=rowb[0:1, 0:tn],
                                            in_=rowb[0:1, 0:tn],
                                            func=AF.Exp, scale=-0.5), drain=True,
                     waits=[(s_c, pbc[0])] if pbc else [])
            vM = vec(lambda h: h.tensor_copy(rowb[0:1, 1024:1024 + tn],
                                             SA[0:1, 0:tn]),
                     waits=[(s_pe, st_pe[-1])]
                     + ([(s_c, pbc[1])] if pbc else []), drain=True)
            bw = [(s_v, ln_cell["app"])] if ln_cell.get("app") else []
            d1 = dma("gpsimd", s_c, lnrow_d[0:1, 0:tn], rowb[0:1, 0:tn],
                     waits=[(s_s, sC)] + ([(s_c, pbc[2])] if pbc else []))
            d2 = dma("gpsimd", s_c, lnrow_d[0:1, 1024:1024 + tn],
                     rowb[0:1, 1024:1024 + tn],
                     waits=[(s_v, vM)] + ([(s_c, pbc[3])] if pbc else []))
            bA = dma("scalar", s_c, abm,
                     lnrow_d[0:1, 0:tn].broadcast_to([128, tn]),
                     waits=[(s_c, d1)] + bw)
            bB = dma("scalar", s_c, abm2,
                     lnrow_d[0:1, 1024:1024 + tn].broadcast_to([128, tn]),
                     waits=[(s_c, d2)])
            ln_cell["bc"] = (d1, d2, bA, bB)
            app_v = []
            for k in range(KC):
                vec(lambda h, k=k: h.tensor_tensor(
                    out=lnT[:, k * T + t0:k * T + t0 + tn],
                    in0=lnT[:, k * T + t0:k * T + t0 + tn],
                    in1=abm2, op=ALU.subtract),
                    waits=[(s_c, bA), (s_c, bB)], drain=True, inc=False)
                v5 = vec(lambda h, k=k: h.tensor_tensor(
                    out=lnT[:, k * T + t0:k * T + t0 + tn],
                    in0=lnT[:, k * T + t0:k * T + t0 + tn],
                    in1=abm, op=ALU.mult), drain=True)
                app_v.append(v5)
            ln_cell["app"] = app_v[-1]
            return app_v

        def emit_gate_z(hf, ln_done, zP=None, extra_pe_waits=(),
                        extra_s_waits=(), extra_v_waits=()):
            zP = P0 if zP is None else zP
            t0 = hf * HALF
            pz = pe(lambda h: [h.matmul(
                zP[0:8, n * 512:(n + 1) * 512], gw_s[:, k * E:(k + 1) * E],
                lnT[:, k * T + t0 + n * 512: k * T + t0 + (n + 1) * 512],
                start=(k == 0), stop=(k == KC - 1))
                for k in range(KC) for n in range(2)][-1],
                waits=[(s_v, max(ln_done)), (s_g, gw_val)] + list(extra_pe_waits))
            sE = sca(lambda h: h.activation(out=rowb[0:8, 0:1024],
                                            in_=zP[0:8, 0:1024],
                                            func=AF.Exp, scale=-1.0),
                     waits=[(s_pe, pz)] + list(extra_s_waits))
            vec(lambda h: h.tensor_scalar(out=rowb[0:8, 0:1024],
                                          in0=rowb[0:8, 0:1024], scalar1=1.0,
                                          scalar2=None, op0=ALU.add),
                waits=[(s_s, sE)] + list(extra_v_waits), drain=True, inc=False)
            v2 = vec(lambda h: h.reciprocal(out=rowb[0:8, 1024:2048],
                                            in_=rowb[0:8, 0:1024]),
                     drain=True)
            return sE, v2

        def gate_tr1(hf, v2, extra=()):
            pt = pe(lambda h: [h.transpose(
                P3[:, t * 8:(t + 1) * 8],
                rowb[0:8, 1024 + t * 128: 1024 + (t + 1) * 128],
                idn_s[0:8, 0:8]) for t in range(8)][-1],
                    waits=[(s_v, v2)] + list(extra))
            v3 = vec(lambda h: h.tensor_copy(gsc[:, 0:64], P3[:, 0:64]),
                     waits=[(s_pe, pt)])

            def r3(a):
                return a.rearrange("p (t e) -> p t e", e=E)

            vec(lambda h: h.tensor_tensor(
                out=r3(gsc[:, 64:128]), in0=r3(gsc[:, 0:64]),
                in1=bcb[:, None, :].broadcast_to([128, 8, E]),
                op=ALU.add), drain=True, inc=False)
            vec(lambda h: h.tensor_reduce(
                out=gmx[:, 0:8], in_=r3(gsc[:, 64:128]), axis=AX.X, op=ALU.max),
                drain=True, inc=False)
            vec(lambda h: h.tensor_tensor(
                out=r3(gsc[:, 128:192]), in0=r3(gsc[:, 64:128]),
                in1=gmx[:, 0:8][:, :, None].broadcast_to([128, 8, E]),
                op=ALU.is_ge), drain=True, inc=False)
            vec(lambda h: h.scalar_tensor_tensor(
                out=r3(gsc[:, 192:256]), in0=r3(gsc[:, 128:192]), scalar=-1e30,
                in1=r3(gsc[:, 64:128]), op0=ALU.mult, op1=ALU.add),
                drain=True, inc=False)
            vec(lambda h: h.tensor_reduce(
                out=gmx[:, 8:16], in_=r3(gsc[:, 192:256]), axis=AX.X, op=ALU.max),
                drain=True, inc=False)
            vec(lambda h: h.tensor_tensor(
                out=r3(gsc[:, 128:192]), in0=r3(gsc[:, 64:128]),
                in1=gmx[:, 8:16][:, :, None].broadcast_to([128, 8, E]),
                op=ALU.is_ge), drain=True, inc=False)
            vA = vec(lambda h: h.scalar_tensor_tensor(
                out=gsc[:, 192:256], in0=gsc[:, 0:64], scalar=SQ5,
                in1=gsc[:, 128:192], op0=ALU.mult, op1=ALU.mult), drain=True)
            return vA, pt

        def gate_tr2(hf, vA):
            t0 = hf * HALF
            pt2 = pe(lambda h: [h.transpose(P3[0:8, t * 128:(t + 1) * 128],
                                            gsc[:, 192 + t * 8:192 + (t + 1) * 8],
                                            idn_s[:, :]) for t in range(8)][-1],
                     waits=[(s_v, vA)])
            vm = vec(lambda h, t0=t0: h.tensor_copy(mT[0:8, t0:t0 + HALF],
                                                    P3[0:8, 0:1024]),
                     waits=[(s_pe, pt2)])
            return vm

        def emit_ffn(first_v_waits=(), nexp=E, phase=3, tail_hook=None,
                     pre_casts=None, sbf_free=None, bf16_w=False,
                     first_w_extra=()):
            pc0 = pre_casts[0] if pre_casts else None
            pc1 = pre_casts[1] if pre_casts else None
            subl_fence = s_v.c
            wu_src0 = wub_d if bf16_w else wu_d
            pre_w = {}
            if phase >= 2:
                for m in (0, 2):
                    wex = first_w_extra if m == 0 else ()
                    pre_w[m] = wstream(
                        wu_src0[0, m:m + 2].rearrange("m p k c -> p m k c"),
                        2048, extra=wex, cast=not bf16_w)
            ln0 = emit_ln(0, HALF, first_v_waits=first_v_waits, pre_casts=pc0,
                          sbf_free=sbf_free)
            ln1 = emit_ln(HALF, HALF, use23=True, pre_casts=pc1,
                          pab_extra=[(s_v, subl_fence)], sbf_free=sbf_free,
                          v_fence_val=subl_fence)
            if phase == 0:
                return max(max(ln0), max(ln1))
            sE0, v20 = emit_gate_z(0, ln0 + [ln1[0]])
            pz1 = pe(lambda h: [h.matmul(
                P2[0:8, n * 512:(n + 1) * 512], gw_s[:, k * E:(k + 1) * E],
                lnT[:, k * T + HALF + n * 512: k * T + HALF + (n + 1) * 512],
                start=(k == 0), stop=(k == KC - 1))
                for k in range(KC) for n in range(2)][-1],
                waits=[(s_v, max(ln1)), (s_g, gw_val)])
            sE1 = sca(lambda h: h.activation(out=rowb[0:8, 0:1024],
                                             in_=P2[0:8, 0:1024],
                                             func=AF.Exp, scale=-1.0),
                      waits=[(s_pe, pz1), (s_v, v20)])
            gate_state = {}

            def g_tr1_0():
                vA0, pt0 = gate_tr1(0, v20, extra=[(s_v, max(ln1))])
                gate_state["vA0"] = vA0
                gate_state["pt0"] = pt0

            def g_tr2_0():
                gate_state["g0"] = gate_tr2(0, gate_state["vA0"])

            def g_sig1():
                vec(lambda h: h.tensor_scalar(out=rowb[0:8, 0:1024],
                                              in0=rowb[0:8, 0:1024], scalar1=1.0,
                                              scalar2=None, op0=ALU.add),
                    waits=[(s_s, sE1), (s_pe, gate_state["pt0"])],
                    drain=True, inc=False)
                gate_state["v21"] = vec(lambda h: h.reciprocal(
                    out=rowb[0:8, 1024:2048], in_=rowb[0:8, 0:1024]),
                    drain=True)

            def g_tr1_1():
                vA1, _pt1 = gate_tr1(1, gate_state["v21"],
                                     extra=[(s_v, gate_state["g0"])])
                gate_state["vA1"] = vA1

            def g_tr2_1():
                gate_state["g1"] = gate_tr2(1, gate_state["vA1"])

            gate_steps = {1: g_tr1_0, 4: g_tr2_0, 5: g_sig1,
                          7: g_tr1_1, 10: g_tr2_1}
            ln_done = max(max(ln0), max(ln1))
            if phase == 1:
                for mm in sorted(gate_steps):
                    gate_steps[mm]()
                return max(gate_state["g0"], gate_state["g1"])

            up_free = {}
            mbc_free = None
            dn_free = {}
            last_vd = None
            for e in range(nexp):
                prev_down_pe = s_pe.c
                wd_val = None
                for hf in range(2):
                    t0 = hf * HALF
                    sq_done = []
                    wu_cols = {}
                    for m in range(FC):
                        if m % 2 == 0:
                            wu_src = wub_d if bf16_w else wu_d
                            wex = (first_w_extra if (e == 0 and hf == 0 and m == 0)
                                   else ())
                            if e == 0 and hf == 0 and m in pre_w:
                                sl, cols, wuv = pre_w.pop(m)
                            else:
                                sl, cols, wuv = wstream(
                                    wu_src[e, m:m + 2].rearrange(
                                        "m p k c -> p m k c"),
                                    2048, extra=wex, cast=not bf16_w)
                            wu_cols = {"sl": sl, "cols": cols, "val": wuv}
                            if hf == 0 and m in (6, 8, 10, 12):
                                q = (m - 6) // 2
                                wdw = [(s_pe, prev_down_pe)]
                                wd_src = wdb_d if bf16_w else wd_d
                                wd_val = dma(
                                    "gpsimd", s_wd,
                                    wdbuf[:, q * 4 * D:(q + 1) * 4 * D].rearrange(
                                        "p (k c) -> p k c", c=D),
                                    wd_src[e, q * 4:(q + 1) * 4].rearrange(
                                        "k p c -> p k c"), waits=wdw)
                        par = m % 2
                        PU = P0 if par == 0 else P1
                        base = wu_cols["cols"] + (m % 2) * 1024
                        mmw = [wu_cols["val"], (s_v, ln_done)]
                        if e == 0 and hf == 0 and m == 0:
                            mmw.append((s_s, sE0))
                        if par in up_free:
                            mmw.append((s_s, up_free[par]))
                        pu = pe(lambda h, PU=PU, base=base, t0=t0: [h.matmul(
                            PU[:, n * 512:(n + 1) * 512],
                            wst[:, base + k * 128: base + (k + 1) * 128],
                            lnT[:, k * T + t0 + n * 512: k * T + t0 + (n + 1) * 512],
                            start=(k == 0), stop=(k == KC - 1))
                            for k in range(KC) for n in range(2)][-1],
                            waits=mmw)
                        if m % 2 == 1:
                            wslot_used(wu_cols["sl"], pu)
                        r1c = (m % 3) * 1024
                        srw = [(s_pe, pu)]
                        if len(sq_done) >= 3:
                            srw.append((s_v, sq_done[-3]))
                        sr = sca(lambda h, PU=PU, r1c=r1c: h.activation(
                            out=sbf[:, r1c:r1c + 1024], in_=PU[:, 0:1024],
                            func=AF.Relu), waits=srw)
                        sq = vec(lambda h, m=m, r1c=r1c: h.tensor_tensor(
                            out=hbuf[:, m * HALF:(m + 1) * HALF],
                            in0=sbf[:, r1c:r1c + 1024],
                            in1=sbf[:, r1c:r1c + 1024], op=ALU.mult),
                            waits=[(s_s, sr)])
                        up_free[par] = sr
                        sq_done.append(sq)
                        if e == 0 and hf == 0 and m in gate_steps:
                            gate_steps[m]()
                    if phase == 2:
                        last_vd = vmb if last_vd is None else last_vd
                        continue
                    gv = gate_state["g0"] if hf == 0 else gate_state["g1"]
                    pmw = [(s_v, gv)]
                    if 1 in up_free:
                        pmw.append((s_s, up_free[1]))
                    if mbc_free is not None:
                        pmw.append((s_v, mbc_free))
                    pm = pe(lambda h, e=e, t0=t0: [h.matmul(
                        P1[:, n * 512:(n + 1) * 512],
                        sel_s[:, e * 128:(e + 1) * 128],
                        mT[0:8, t0 + n * 512: t0 + (n + 1) * 512],
                        start=True, stop=True) for n in range(2)][-1],
                        waits=pmw)
                    smb = sca(lambda h: h.activation(out=mbc[:], in_=P1[:, 0:1024],
                                                     func=AF.Square),
                              waits=[(s_pe, pm)])
                    for md in range(KC):
                        par = md % 2
                        PD = P2 if par == 0 else P3
                        mmw = [(s_v, sq_done[11] if md == 0 else sq_done[-1])]
                        if md < 2:
                            mmw += [(s_wd, wd_val), (s_v, gv)]
                        if par in dn_free:
                            mmw.append((s_v, dn_free[par]))
                        ksplit = 12 if md == 0 else FC
                        pd = pe(lambda h, md=md, PD=PD, t0=t0, ks=ksplit: [h.matmul(
                            PD[:, n * 512:(n + 1) * 512],
                            wdbuf[:, k * D + md * 128: k * D + md * 128 + 128],
                            hbuf[:, k * HALF + n * 512: k * HALF + (n + 1) * 512],
                            start=(k == 0), stop=(k == FC - 1))
                            for k in range(ks) for n in range(2)][-1],
                            waits=mmw)
                        if md == 0:
                            pd = pe(lambda h, md=md, PD=PD, t0=t0: [h.matmul(
                                PD[:, n * 512:(n + 1) * 512],
                                wdbuf[:, k * D + md * 128: k * D + md * 128 + 128],
                                hbuf[:, k * HALF + n * 512: k * HALF + (n + 1) * 512],
                                start=(k == 0), stop=(k == FC - 1))
                                for k in range(12, FC) for n in range(2)][-1],
                                waits=[(s_v, sq_done[-1])])
                        vec(lambda h, PD=PD: h.tensor_tensor(
                            out=PD[:, 0:1024], in0=PD[:, 0:1024], in1=mbc[:],
                            op=ALU.mult), waits=[(s_pe, pd), (s_s, smb)],
                            inc=False)
                        vd = vec(lambda h, md=md, PD=PD, t0=t0: h.scalar_tensor_tensor(
                            out=res[:, md * T + t0: md * T + t0 + 1024],
                            in0=PD[:, 0:1024], scalar=1.0,
                            in1=res[:, md * T + t0: md * T + t0 + 1024],
                            op0=ALU.mult, op1=ALU.add), drain=True,
                            waits=[])
                        dn_free[par] = vd
                        last_vd = vd
                        mbc_free = vd
                        if tail_hook is not None and e == nexp - 1:
                            tail_hook(hf, md, vd)
                    emit_ffn.sbf_last = sq_done[-1]
            return last_vd

        def emit_attn(first_v_waits=(), phase=4, ngrp=NGRP,
                      tail_hook=None, pre_casts=None, sbf_free=None,
                      bf16_w=False, group_hook=None, first_w_extra=()):
            wq_src = wqb_d if bf16_w else wq_d
            wk_src = wkb_d if bf16_w else wk_d
            wq_pv, wk_pv = [], []
            for q in range(4):
                gw_ = ([(s_pe, s_pe.c)] + list(first_w_extra)) if q == 0 else []
                wq_pv.append(dma(
                    "sync" if bf16_w else "gpsimd", s_wd,
                    wdbuf[:, q * 2048:(q + 1) * 2048].rearrange(
                        "p (m c) -> p m c", c=1024),
                    wq_src[2 * q:2 * q + 2].rearrange("m p c -> p m c"),
                    waits=gw_))
            for q in range(4):
                gw_ = ([(s_pe, s_pe.c)] + list(first_w_extra)) if q == 0 else []
                wk_pv.append(dma(
                    "gpsimd", s_wk,
                    wdbuf[:, 8192 + q * 2048:8192 + (q + 1) * 2048].rearrange(
                        "p (m c) -> p m c", c=1024),
                    wk_src[2 * q:2 * q + 2].rearrange("m p c -> p m c"),
                    waits=gw_))
            wkc_val = wk_pv[-1]
            vec(lambda h: h.memset(S_all[:], 0.0), waits=[(s_v, s_v.c)], inc=False)
            sbf_ready = vec(lambda h: h.memset(S_bf[:], 0.0))
            slot_free = {}
            SLOTS = [(P0, 0), (P0, 512), (P1, 0), (P1, 512)]
            sl_i = {"i": 0}

            def proj_slot():
                i = sl_i["i"] % 4
                sl_i["i"] += 1
                return i, SLOTS[i]

            ret_done = None
            pending = []
            scev_cell = {"v": None}

            def emit_oproj(t0, ret_gate, hook=None):
                rd = None
                for m0 in range(0, KC, 2):
                    sl, cols, wov = wstream(
                        wob_d[m0:m0 + 2].rearrange("m p c -> p m c"), 2048,
                        extra=[(s_pc0, wob_val)] if m0 == 0 else ())
                    last_po = None
                    for mi in range(2):
                        m = m0 + mi
                        si, (PT, off) = proj_slot()
                        mmw = [wov, (s_v, ret_gate)]
                        if si in slot_free:
                            mmw.append((s_v, slot_free[si]))
                        po = pe(lambda h, cols=cols, mi=mi, PT=PT, off=off: [h.matmul(
                            PT[:, off:off + GRP],
                            wst[:, cols + mi * 1024 + k * 128:
                                cols + mi * 1024 + (k + 1) * 128],
                            hbuf[:, A_RET + k * GRP: A_RET + (k + 1) * GRP],
                            start=(k == 0), stop=(k == KC - 1))
                            for k in range(KC)][-1], waits=mmw)
                        last_po = po
                        vo = vec(lambda h, PT=PT, off=off, m=m, t0=t0:
                                 h.scalar_tensor_tensor(
                                     out=res[:, m * T + t0: m * T + t0 + GRP],
                                     in0=PT[:, off:off + GRP], scalar=1.0,
                                     in1=res[:, m * T + t0: m * T + t0 + GRP],
                                     op0=ALU.mult, op1=ALU.add), waits=[(s_pe, po)])
                        slot_free[si] = vo
                        rd = vo
                        if hook is not None:
                            hook(m, vo)
                    wslot_used(sl, last_po)
                return rd

            pr_prev, pst_prev = s_pe.c, s_pe.c
            pc0 = pre_casts[0] if pre_casts else None
            pc1 = pre_casts[1] if pre_casts else None
            attn_fence = s_v.c
            lnA = emit_ln(0, HALF, first_v_waits=first_v_waits, pre_casts=pc0,
                          sbf_free=sbf_free, v_fence=False)
            lnB = emit_ln(HALF, HALF, use23=True, pre_casts=pc1,
                          sbf_free=sbf_free,
                          pab_extra=[(s_v, attn_fence)],
                          v_fence_val=attn_fence)
            ln_done = max(max(lnA), max(lnB))
            for g in range(ngrp):
                t0 = g * GRP
                if group_hook is not None:
                    group_hook(g)
                if phase == 0:
                    ret_done = ln_done
                    continue
                if pending:
                    pg, pt0, prg = pending.pop(0)
                    hook = None
                    if tail_hook is not None and pg in (3, ngrp - 1):
                        hook = (lambda m, vo, pg=pg: tail_hook(
                            0 if pg == 3 else 1, m, vo))
                    ret_done = emit_oproj(pt0, prg, hook=hook)
                qkT_ev = []
                for m in range(KC):
                    for coff, dst in ((0, A_QT), (8192, A_KT)):
                        si, (PT, off) = proj_slot()
                        wsem = (s_wd, wq_pv[m // 2]) if coff == 0 \
                            else (s_wk, wk_pv[m // 2])
                        mmw = [wsem, (s_v, ln_done)]
                        if si in slot_free:
                            mmw.append((s_v, slot_free[si]))
                        pq = pe(lambda h, coff=coff, m=m, PT=PT, off=off,
                                t0=t0: [h.matmul(
                                    PT[:, off:off + GRP],
                                    wdbuf[:, coff + m * 1024 + k * 128:
                                          coff + m * 1024 + (k + 1) * 128],
                                    lnT[:, k * T + t0: k * T + t0 + GRP],
                                    start=(k == 0), stop=(k == KC - 1))
                                    for k in range(KC)][-1], waits=mmw)
                        vq = vec(lambda h, PT=PT, off=off, dst=dst, m=m:
                                 h.tensor_copy(
                                     hbuf[:, dst + m * GRP: dst + (m + 1) * GRP],
                                     PT[:, off:off + GRP]), waits=[(s_pe, pq)])
                        slot_free[si] = vq
                        qkT_ev.append(vq)
                qkT_done = max(qkT_ev)
                tokev = []
                for c in range(2):
                    PT2 = P2 if c == 0 else P3
                    tw = ([(s_v, qkT_ev[-1]),
                           (s_v, s_v.c if g == 0 else sbf_ready)]
                          if c == 0 else [])
                    ptr = pe(lambda h, c=c, PT2=PT2: [h.matmul(
                        PT2[:, m * 128:(m + 1) * 128],
                        hbuf[:, A_KT + m * GRP + c * 128:
                             A_KT + m * GRP + c * 128 + 128],
                        idnb_s[:], start=True, stop=True)
                        for m in range(KC)][-1], waits=tw)
                    vv = vec(lambda h, c=c, PT2=PT2: h.tensor_copy(
                        hbuf[:, A_KTOK + c * D: A_KTOK + (c + 1) * D],
                        PT2[:, 0:1024]), waits=[(s_pe, ptr)])
                    tokev.append(vv)
                mm_last = None
                for k0 in range(0, KC, 2):
                    sl, cols, wkv = wstream(
                        wvb_d[k0:k0 + 2].rearrange("k p c -> p k c"), 2048,
                        extra=[(s_pc0, wvb_val)] if (g == 0 and k0 == 0) else ())
                    last_pv = None
                    for ki in range(2):
                        k = k0 + ki
                        mmw = [wkv, (s_v, ln_done)]
                        if k == 0:
                            mmw.append((s_v, max(tokev)))
                        pv = pe(lambda h, k=k, cols=cols, ki=ki, t0=t0: [h.matmul(
                            (P2 if t == 0 else P3)[:, n * 512:(n + 1) * 512],
                            lnT[:, k * T + t0 + t * 128:
                                k * T + t0 + t * 128 + 128],
                            wst[:, cols + ki * 1024 + n * 512:
                                cols + ki * 1024 + (n + 1) * 512],
                            start=(k == 0), stop=(k == KC - 1))
                            for t in range(2) for n in range(2)][-1],
                            waits=mmw)
                        last_pv = pv
                        mm_last = pv
                    wslot_used(sl, last_pv)
                for t in range(2):
                    vv = vec(lambda h, t=t: h.tensor_copy(
                        hbuf[:, A_V + t * D: A_V + (t + 1) * D],
                        (P2 if t == 0 else P3)[:, 0:1024]),
                        waits=[(s_pe, mm_last)])
                    tokev.append(vv)
                tok_done = max(tokev)
                if phase == 2:
                    ret_done = tok_done
                    continue
                QP_OFF = (A_QP, 14336)
                KP_OFF = (A_KP, 15360)
                sc_pre = []
                for c in range(2):
                    ct = c * 128
                    w = [(s_v, qkT_done)]
                    if c == 0:
                        w += [(s_pe, pr_prev), (s_pe, pst_prev)]
                    vqp = vec(lambda h, ct=ct, c=c: h.tensor_tensor(
                        out=hbuf[:, QP_OFF[c]:QP_OFF[c] + 1024].rearrange(
                            "p (k c2) -> p k c2", c2=128),
                        in0=hbuf[:, A_QT:A_QT + 2048].rearrange(
                            "p (k c2) -> p k c2", c2=GRP)[:, :, ct:ct + 128],
                        in1=grw[:].rearrange("p (k c2) -> p k c2", c2=128),
                        op=ALU.mult), waits=w)
                    vkp = vec(lambda h, c=c: h.tensor_tensor(
                        out=hbuf[:, KP_OFF[c]:KP_OFF[c] + 1024],
                        in0=hbuf[:, A_KTOK + c * D: A_KTOK + (c + 1) * D],
                        in1=kdc[:], op=ALU.mult),
                        waits=[(s_v, tok_done)])
                    sc_pre.append((vqp, vkp))
                ret_ev = []
                sub = ATTN_SUB
                for c in range(2):
                    ct = c * 128
                    vqp, vkp = sc_pre[c]
                    if sub == 0:
                        ret_ev = [vqp, vkp]
                        continue
                    sc_ev = []
                    for hs in range(2):
                        mmw = [(s_v, qkT_done)]
                        if hs == 0 and c == 0:
                            mmw.append((s_v, tok_done))
                        if hs == 0 and scev_cell["v"] is not None:
                            mmw.append((s_v, scev_cell["v"]))
                        if sc_ev:
                            mmw.append((s_v, sc_ev[-1]))
                        ps = pe(lambda h, hs=hs, ct=ct: [h.matmul(
                            P2[:, sccol(h_i):sccol(h_i) + 128],
                            hbuf[:, A_KT + (h_i // 2) * GRP + ct:
                                 A_KT + (h_i // 2) * GRP + ct + 128][
                                (h_i % 2) * 64:(h_i % 2) * 64 + 64, :],
                            hbuf[:, A_QT + (h_i // 2) * GRP + ct:
                                 A_QT + (h_i // 2) * GRP + ct + 128][
                                (h_i % 2) * 64:(h_i % 2) * 64 + 64, :],
                            start=True, stop=True)
                            for h_i in range(hs * 8, hs * 8 + 8)][-1],
                            waits=mmw)
                        vsc = vec(lambda h, hs=hs: h.tensor_tensor(
                            out=hbuf[:, A_SC + hs * 1024: A_SC + (hs + 1) * 1024],
                            in0=P2[:, 0:1024],
                            in1=dct[:, hs * 1024:(hs + 1) * 1024], op=ALU.mult),
                            waits=[(s_pe, ps)])
                        sc_ev.append(vsc)
                    scev_cell["v"] = sc_ev[-1]
                    if sub == 1:
                        ret_ev = [max(sc_ev)]
                        continue
                    if sub >= 4:
                        pst = pe(lambda h, c=c: [h.matmul(
                            P2[(h_i % 2) * 64:(h_i % 2) * 64 + 64,
                               (h_i // 2) * 64:(h_i // 2) * 64 + 64],
                            hbuf[:, KP_OFF[c] + h_i * 64: KP_OFF[c] + h_i * 64 + 64],
                            hbuf[:, A_V + c * D + h_i * 64:
                                 A_V + c * D + h_i * 64 + 64],
                            start=True, stop=True,
                            tile_position=(0, (h_i % 2) * 64)) for h_i in range(H)][-1],
                            waits=[(s_v, max(vkp, sc_ev[-1]))])
                        pst_prev = pst
                        vec(lambda h: h.tensor_tensor(
                            out=S_all[:], in0=S_all[:], in1=gcs[:], op=ALU.mult),
                            drain=True, inc=False)
                        s_add = vec(lambda h: h.tensor_tensor(
                            out=S_all[:], in0=S_all[:], in1=P2[:, 0:512],
                            op=ALU.add), waits=[(s_pe, pst)], drain=True)
                        scev_cell["v"] = max(scev_cell["v"], s_add)
                    mmw = [(s_v, max(sc_ev[-1], vqp, sbf_ready))]
                    if c == 0:
                        mmw.append((s_v, tok_done))
                    if ret_ev:
                        mmw.append((s_v, ret_ev[-1]))
                    def retfn(h, c=c, sub=sub):
                        last = None
                        for h_i in range(H):
                            rr = (h_i % 2) * 64
                            pcol = (h_i // 2)
                            mm1 = h.matmul(
                                P3[rr:rr + 64, pcol * 128:pcol * 128 + 128],
                                hbuf[:, A_V + c * D + h_i * 64:
                                     A_V + c * D + h_i * 64 + 64],
                                hbuf[:, A_SC + (h_i // 8) * 1024 + sccol(h_i):
                                     A_SC + (h_i // 8) * 1024 + sccol(h_i) + 128],
                                start=True, stop=(sub == 2), tile_position=(0, rr))
                            if sub == 2:
                                last = mm1
                                continue
                            last = h.matmul(
                                P3[rr:rr + 64, pcol * 128:pcol * 128 + 128],
                                (S_bf if c == 0 else S_bf2)[
                                    rr:rr + 64, pcol * 64:pcol * 64 + 64],
                                hbuf[:, QP_OFF[c] + pcol * 128:
                                     QP_OFF[c] + pcol * 128 + 128][rr:rr + 64, :],
                                start=False, stop=True, tile_position=(rr, rr))
                        return last

                    pr = pe(retfn, waits=mmw)
                    pr_prev = pr
                    vre = vec(lambda h, ct=ct: h.tensor_copy(
                        hbuf[:, A_RET:A_RET + 2048].rearrange(
                            "p (k c2) -> p k c2", c2=GRP)[:, :, ct:ct + 128],
                        P3[:, 0:1024].rearrange("p (k c2) -> p k c2", c2=128)),
                        waits=[(s_pe, pr)])
                    ret_ev.append(vre)
                    if sub == 3:
                        continue
                    sbf_ready = vec(lambda h, c=c: h.tensor_copy(
                        (S_bf2 if c == 0 else S_bf)[:], S_all[:]), drain=True)
                if phase == 3:
                    ret_done = max(ret_ev)
                    continue
                pending.append((g, t0, max(ret_ev)))
            emit_attn.sbf_last = ln_done
            while pending:
                pg, pt0, prg = pending.pop(0)
                hook = None
                if tail_hook is not None and pg in (3, ngrp - 1):
                    hook = (lambda m, vo, pg=pg: tail_hook(0 if pg == 3 else 1,
                                                           m, vo))
                ret_done = emit_oproj(pt0, prg, hook=hook)
            return ret_done

        last_store = 0
        resv_k = [(s_x, res_ready)] * KC
        for b in range(n_batches):
            done = s_v.c
            fw = [[w] for w in resv_k]
            full = sublayers == (0, 1, 2) and FFN_PHASE == 3 and ATTN_PHASE == 4
            attn_pc = [[None] * KC, [None] * KC]
            ffn2_pc = [[None] * KC, [None] * KC]

            def ffn1_hook(hf, md, vd):
                attn_pc[hf][md] = emit_ln_cast(md, hf * HALF, HALF,
                                               waits=[(s_v, vd)])

            def attn_hook(hf, m, vo):
                ffn2_pc[hf][m] = emit_ln_cast(m, hf * HALF, HALF,
                                              waits=[(s_v, vo)])

            next_resv = [None] * KC

            def ffn2_hook(hf, md, vd, b=b):
                if hf != 1:
                    return
                vfk = vec(lambda h: h.tensor_copy(gsc[:, 8:16], gsc[:, 8:16]),
                          waits=[(s_v, vd)], drain=True)
                st = dma("sync", s_ch[md],
                         out_d[b].rearrange("(k p) t -> k p t", p=128)[md],
                         res[:, md * T:(md + 1) * T], waits=[(s_v, vfk)])
                if b + 1 < n_batches:
                    ld = dma("sync", s_ch[md], res[:, md * T:(md + 1) * T],
                             xT[b + 1].rearrange("(k p) t -> k p t", p=128)[md],
                             waits=[(s_ch[md], st)])
                    next_resv[md] = (s_ch[md], ld)

            def cast_hook(g):
                if g < E:
                    dma("gpsimd", s_pc, wub_d[g, 0:8], wu_d[g, 0:8])
                    dma("gpsimd", s_pc, wub_d[g, 8:16], wu_d[g, 8:16])
                    dma("gpsimd", s_pc, wdb_d[g, 0:8], wd_d[g, 0:8])
                    dma("gpsimd", s_pc, wdb_d[g, 8:16], wd_d[g, 8:16])
                if g == 6:
                    dma("gpsimd", s_pc, wqb_d, wq_d)
                    dma("gpsimd", s_pc, wkb_d, wk_d)

            bf16w = full and b > 0
            if 0 in sublayers:
                done = emit_ffn(first_v_waits=fw, nexp=FFN_NEXP, phase=FFN_PHASE,
                                tail_hook=ffn1_hook if full else None,
                                bf16_w=bf16w,
                                first_w_extra=[(s_pc, s_pc.c)] if bf16w else ())
                fw = []
            if 1 in sublayers:
                done = emit_attn(first_v_waits=fw, phase=ATTN_PHASE,
                                 ngrp=ATTN_NGRP,
                                 tail_hook=attn_hook if full else None,
                                 pre_casts=attn_pc if full else None,
                                 sbf_free=getattr(emit_ffn, "sbf_last", None)
                                 if full else None,
                                 bf16_w=bf16w,
                                 group_hook=cast_hook if (full and b == 0)
                                 else None,
                                 first_w_extra=[(s_pc, s_pc.c)] if bf16w else ())
                fw = []
            if 2 in sublayers:
                done = emit_ffn(first_v_waits=fw,
                                pre_casts=ffn2_pc if full else None,
                                sbf_free=getattr(emit_attn, "sbf_last", None)
                                if full else None,
                                bf16_w=(full and True),
                                first_w_extra=[(s_pc, s_pc.c)]
                                if (full and b == 0) else (),
                                tail_hook=ffn2_hook if full else None)
                fw = []
            if full:
                if b + 1 < n_batches:
                    assert all(v is not None for v in next_resv)
                    resv_k = next_resv
            else:
                vf = vec(lambda h: h.tensor_copy(gsc[:, 0:8], gsc[:, 0:8]),
                         waits=[(s_v, done)], drain=True)
                new_resv = []
                st_vals = []
                for k in range(KC):
                    st_k = dma("sync", s_ch[k],
                               out_d[b].rearrange("(k p) t -> k p t", p=128)[k],
                               res[:, k * T:(k + 1) * T], waits=[(s_v, vf)])
                    st_vals.append(st_k)
                for k in range(KC):
                    if b + 1 < n_batches:
                        ld_k = dma("sync", s_ch[k], res[:, k * T:(k + 1) * T],
                                   xT[b + 1].rearrange("(k p) t -> k p t",
                                                       p=128)[k],
                                   waits=[(s_ch[k], st_vals[k])])
                        new_resv.append((s_ch[k], ld_k))
                if new_resv:
                    resv_k = new_resv
        fin = s_x.c
        fin_ch = [s.c for s in s_ch]

        def _(h, fin=fin, fin_ch=tuple(fin_ch), sems=tuple(s.h for s in s_ch)):
            h.wait_ge(s_x.h, fin)
            for sh, fv in zip(sems, fin_ch):
                h.wait_ge(sh, fv)

        em.add("sync", _)
        em.flush()
    nc.finalize()
    return nc


def _prepack(inputs):
    x = np.asarray(inputs["L0_attn_x"], dtype=np.float32)
    wq = np.asarray(inputs["L0_attn_wq"], dtype=np.float32)
    wk = np.asarray(inputs["L0_attn_wk"], dtype=np.float32)
    wv = np.asarray(inputs["L0_attn_wv"], dtype=np.float32)
    wo = np.asarray(inputs["L0_attn_wo"], dtype=np.float32)
    gam = np.asarray(inputs["L0_attn_gammas"], dtype=np.float32)
    gw = np.asarray(inputs["L0_ffn_gate_w"], dtype=np.float32)
    wu = np.asarray(inputs["L0_ffn_w_up"], dtype=np.float32)
    wd = np.asarray(inputs["L0_ffn_w_down"], dtype=np.float32)
    eb = np.asarray(inputs["L0_ffn_expert_bias"], dtype=np.float32)

    def mmaj(w):
        return np.ascontiguousarray(
            w.reshape(KC, 128, KC, 128).transpose(0, 3, 2, 1).reshape(KC, 128, D))

    def kmaj(w):
        return np.ascontiguousarray(w.reshape(D, KC, 128).transpose(1, 2, 0))

    packs = {
        "wu_p": np.ascontiguousarray(
            wu.reshape(E, FC, 128, KC, 128).transpose(0, 1, 4, 3, 2)),
        "wd_p": np.ascontiguousarray(
            wd.reshape(E, D, FC, 128).transpose(0, 2, 3, 1)),
        "wq_m": mmaj(wq), "wk_m": mmaj(wk), "wo_m": mmaj(wo),
        "wk_k": kmaj(wk), "wv_k": kmaj(wv),
        "gw_p": np.ascontiguousarray(gw.reshape(E, KC, 128).transpose(1, 2, 0)),
        "ebias": np.ascontiguousarray(eb.reshape(1, E)),
        "gam": np.ascontiguousarray(gam.reshape(1, H)),
    }
    tau = np.arange(128, dtype=np.float32)
    consts = {
        "etri": np.maximum(tau[None, :] - tau[:, None], 0.0).astype(ml_dtypes.bfloat16),
        "tri": ((tau[None, :] >= tau[:, None]) * 0.125).astype(ml_dtypes.bfloat16),
        "erow": np.tile(tau[None, :] + 1.0, (128, 1)).astype(ml_dtypes.bfloat16),
        "ecol": np.ascontiguousarray((127.0 - tau).reshape(128, 1)),
        "idn": np.eye(128, dtype=np.float32),
        "idnb": np.eye(128, dtype=ml_dtypes.bfloat16),
        "ones1": np.ones((1, 128), dtype=np.float32),
        "onesb": np.full((128, 1), 1.0 / D, dtype=ml_dtypes.bfloat16),
        "sel_p": np.ascontiguousarray(
            np.eye(E, dtype=ml_dtypes.bfloat16)[:, :, None].repeat(128, 2)),
        "cvals": np.array([[1e-5, np.log(0.125), 1.0, 0, 0, 0, 0, 0]], np.float32),
    }
    xt = np.ascontiguousarray(x.reshape(NCORE, BPC, T, D).transpose(0, 1, 3, 2))
    return xt, packs, consts


def kernel(**inputs):
    xt, packs, consts = _prepack(inputs)
    nc = build()
    in_maps = []
    for c in range(NCORE):
        m = {"xT": xt[c]}
        m.update(packs)
        m.update(consts)
        in_maps.append(m)
    res = run_bass_kernel_spmd(nc, in_maps, core_ids=list(range(NCORE)))
    out = np.stack([r["outT"] for r in res.results])
    out = out.reshape(NCORE * BPC, D, T).transpose(0, 2, 1)
    return np.ascontiguousarray(out)


# version 30
# speedup vs baseline: 1.1189x; 1.1079x over previous
import numpy as np
import ml_dtypes
from contextlib import ExitStack

import concourse.bass as bass
import concourse.mybir as mybir
from concourse.bass_utils import run_bass_kernel_spmd

F32 = mybir.dt.float32
BF16 = mybir.dt.bfloat16
ALU = mybir.AluOpType
AF = mybir.ActivationFunctionType
AX = mybir.AxisListType

D = 1024
T = 2048
KC = 8
FC = 16
E = 8
F = 2048
H = 16
HD = 64
GRP = 256
NGRP = T // GRP
HALF = 1024
NCORE = 8
BPC = 4
SQ5 = float(np.sqrt(0.5))

ENGINES = ("sync", "gpsimd", "tensor", "vector", "scalar")


class Sem:
    def __init__(self, ctx, nc, name):
        self.h = ctx.enter_context(nc.semaphore(name))
        self.c = 0


class Emitter:

    def __init__(self, block):
        self.block = block
        self.q = {e: [] for e in ENGINES}

    def add(self, eng, fn):
        self.q[eng].append(fn)

    def flush(self):
        for eng in ENGINES:
            fns = self.q[eng]
            if not fns:
                continue

            def run(h, fns=fns):
                for f in fns:
                    f(h)

            getattr(self.block, eng)(run)
        self.q = {e: [] for e in ENGINES}


def build(n_batches=BPC, sublayers=(0, 1, 2), FFN_NEXP=E, FFN_PHASE=3,
          ATTN_PHASE=4, ATTN_NGRP=NGRP, ATTN_SUB=4):
    nc = bass.Bass("TRN2", target_bir_lowering=False)

    xT = nc.dram_tensor("xT", [BPC, D, T], F32, kind="ExternalInput").ap()
    out_d = nc.dram_tensor("outT", [BPC, D, T], F32, kind="ExternalOutput").ap()
    wu_d = nc.dram_tensor("wu_p", [E, FC, 128, KC, 128], F32, kind="ExternalInput").ap()
    wd_d = nc.dram_tensor("wd_p", [E, FC, 128, D], F32, kind="ExternalInput").ap()
    wq_d = nc.dram_tensor("wq_m", [KC, 128, D], F32, kind="ExternalInput").ap()
    wk_d = nc.dram_tensor("wk_m", [KC, 128, D], F32, kind="ExternalInput").ap()
    wo_d = nc.dram_tensor("wo_m", [KC, 128, D], F32, kind="ExternalInput").ap()
    wkk_d = nc.dram_tensor("wk_k", [KC, 128, D], F32, kind="ExternalInput").ap()
    wvk_d = nc.dram_tensor("wv_k", [KC, 128, D], F32, kind="ExternalInput").ap()
    gw_d = nc.dram_tensor("gw_p", [KC, 128, E], F32, kind="ExternalInput").ap()
    bias_d = nc.dram_tensor("ebias", [1, E], F32, kind="ExternalInput").ap()
    gam_d = nc.dram_tensor("gam", [1, H], F32, kind="ExternalInput").ap()
    etri_d = nc.dram_tensor("etri", [128, 128], BF16, kind="ExternalInput").ap()
    tri_d = nc.dram_tensor("tri", [128, 128], BF16, kind="ExternalInput").ap()
    erow_d = nc.dram_tensor("erow", [128, 128], BF16, kind="ExternalInput").ap()
    ecol_d = nc.dram_tensor("ecol", [128, 1], F32, kind="ExternalInput").ap()
    idn_d = nc.dram_tensor("idn", [128, 128], F32, kind="ExternalInput").ap()
    ones1_d = nc.dram_tensor("ones1", [1, 128], F32, kind="ExternalInput").ap()
    onesb_d = nc.dram_tensor("onesb", [128, 1], BF16, kind="ExternalInput").ap()
    idnb_d = nc.dram_tensor("idnb", [128, 128], BF16, kind="ExternalInput").ap()
    sel_d = nc.dram_tensor("sel_p", [E, E, 128], BF16, kind="ExternalInput").ap()
    cst_d = nc.dram_tensor("cvals", [1, 8], F32, kind="ExternalInput").ap()
    lnrow_d = nc.dram_tensor("lnrow", [1, 2048], BF16, kind="Internal").ap()
    wvb_d = nc.dram_tensor("wv_b16", [KC, 128, D], BF16, kind="Internal").ap()
    wob_d = nc.dram_tensor("wo_b16", [KC, 128, D], BF16, kind="Internal").ap()
    wqb_d = nc.dram_tensor("wq_b16", [KC, 128, D], BF16, kind="Internal").ap()
    wkb_d = nc.dram_tensor("wk_b16", [KC, 128, D], BF16, kind="Internal").ap()
    wub_d = nc.dram_tensor("wu_b16", [E, FC, 128, KC, 128], BF16,
                           kind="Internal").ap()
    wdb_d = nc.dram_tensor("wd_b16", [E, FC, 128, D], BF16, kind="Internal").ap()

    ctx = ExitStack()
    with ctx:
        sbt = nc.sbuf_tensor
        res = ctx.enter_context(sbt("res", [128, KC * T], F32))
        lnT = ctx.enter_context(sbt("lnT", [128, KC * T], BF16))
        hbuf = ctx.enter_context(sbt("hbuf", [128, FC * HALF], BF16))
        wdbuf = ctx.enter_context(sbt("wdbuf", [128, FC * D], BF16))
        wst = ctx.enter_context(sbt("wst", [128, 2 * 2048], BF16))
        sbf = ctx.enter_context(sbt("sbf", [128, 4096], BF16))
        gsc = ctx.enter_context(sbt("gsc", [128, 256], F32))
        rowb = ctx.enter_context(sbt("rowb", [8, 2048], F32))
        mbc = sbf[:, 3072:4096]
        mT = ctx.enter_context(sbt("mT", [8, T], BF16))
        gmx = ctx.enter_context(sbt("gmx", [128, 32], F32))
        dct = ctx.enter_context(sbt("dct", [128, H * 128], BF16))
        grw = ctx.enter_context(sbt("grw", [128, KC * 128], BF16))
        kdc = ctx.enter_context(sbt("kdc", [128, D], BF16))
        gcs = ctx.enter_context(sbt("gcs", [128, 512], BF16))
        lgb = ctx.enter_context(sbt("lgb", [128, H], F32))
        bcb = ctx.enter_context(sbt("bcb", [128, E], F32))
        expc = ctx.enter_context(sbt("expc", [128, 1], F32))
        idn_s = ctx.enter_context(sbt("idn_s", [128, 128], F32))
        idnb_s = ctx.enter_context(sbt("idnb_s", [128, 128], BF16))
        onesb = ctx.enter_context(sbt("onesbf", [128, 1], BF16))
        sel_s = ctx.enter_context(sbt("sel_s", [E, E * 128], BF16))
        gw_s = ctx.enter_context(sbt("gw_s", [128, KC * E], BF16))
        cst = ctx.enter_context(sbt("cst_s", [128, 8], F32))
        tri_s = ctx.enter_context(sbt("tri_s", [128, 128], BF16))
        etri_s = ctx.enter_context(sbt("etri_s", [128, 128], BF16))
        S_bf2 = ctx.enter_context(sbt("S_bf2", [128, 512], BF16))
        erow_s = sbf[:, 0:128]
        tf32 = gsc[:, 0:128]
        S_all = ctx.enter_context(sbt("S_all", [128, 512], F32))
        S_bf = ctx.enter_context(sbt("S_bf", [128, 512], BF16))

        P0 = ctx.enter_context(nc.psum_tensor("P0", [128, 1024], F32))
        P1 = ctx.enter_context(nc.psum_tensor("P1", [128, 1024], F32))
        P2 = ctx.enter_context(nc.psum_tensor("P2", [128, 1024], F32))
        P3 = ctx.enter_context(nc.psum_tensor("P3", [128, 1024], F32))

        s_x = Sem(ctx, nc, "s_x")
        s_c = Sem(ctx, nc, "s_c")
        s_g = Sem(ctx, nc, "s_g")
        s_wd = Sem(ctx, nc, "s_wd")
        s_wk = Sem(ctx, nc, "s_wk")
        s_w0 = Sem(ctx, nc, "s_w0")
        s_w1 = Sem(ctx, nc, "s_w1")
        s_w2 = Sem(ctx, nc, "s_w2")
        s_ch = [Sem(ctx, nc, f"s_ch{k}") for k in range(KC)]
        s_pc = Sem(ctx, nc, "s_pc")
        s_pc0 = Sem(ctx, nc, "s_pc0")
        s_pe = Sem(ctx, nc, "s_pe")
        s_v = Sem(ctx, nc, "s_v")
        s_s = Sem(ctx, nc, "s_s")

        nc.const_aps.register(F32, 1e-5, cst[:, 0:1])
        nc.const_aps.register(F32, float(np.log(0.125)), cst[:, 1:2])
        nc.const_aps.register(F32, 1.0, cst[:, 2:3])

        block = ctx.enter_context(nc.Block())
        em = Emitter(block)

        def dma(queue, sem, out, in_, waits=()):
            sem.c += 16
            val = sem.c

            def _(h, out=out, in_=in_, waits=tuple(waits), val=val):
                for ws, wv in waits:
                    h.wait_ge(ws.h, wv)
                h.dma_start(out=out, in_=in_).then_inc(sem.h, 16)

            em.add(queue, _)
            return val

        def _mk(sem, eng):
            def emitter(fn, waits=(), inc=True, drain=False):
                val = None
                if inc:
                    sem.c += 1
                    val = sem.c

                def _(h, fn=fn, waits=tuple(waits), val=val, drain=drain):
                    for ws, wv in waits:
                        h.wait_ge(ws.h, wv)
                    if drain:
                        h.drain()
                    i = fn(h)
                    if val is not None:
                        i.then_inc(sem.h, 1)

                em.add(eng, _)
                return val

            return emitter

        pe = _mk(s_pe, "tensor")
        vec = _mk(s_v, "vector")
        sca = _mk(s_s, "scalar")


        def sccol(h_i):
            return (h_i % 2) * 512 + ((h_i % 8) // 2) * 128
        A_QT, A_KT, A_KTOK, A_V = 0, 2048, 4096, 6144
        A_SC, A_RET, A_QP, A_KP = 8192, 10240, 12288, 13312

        dma("sync", s_c, idn_s[:], idn_d)
        dma("sync", s_c, onesb[:], onesb_d)
        dma("sync", s_c, cst[:], cst_d.broadcast_to([128, 8]))
        dma("sync", s_c, etri_s[:], etri_d)
        dma("sync", s_c, tri_s[:], tri_d)
        dma("sync", s_c, erow_s[:], erow_d)
        dma("sync", s_c, idnb_s[:], idnb_d)
        dma("sync", s_c, expc[:], ecol_d)
        dma("sync", s_c, lgb[:], gam_d.broadcast_to([128, H]))
        dma("sync", s_c, bcb[:], bias_d.broadcast_to([128, E]))
        dma("sync", s_c, sel_s[:], sel_d.rearrange("j e p -> j (e p)"))
        gw_val = dma("gpsimd", s_g, gw_s[:].rearrange("p (k e) -> p k e", e=E), gw_d.rearrange("k p e -> p k e"))
        wvb_val = dma("gpsimd", s_pc0, wvb_d, wvk_d)
        wob_val = dma("gpsimd", s_pc0, wob_d, wo_d)
        wvb_val = wob_val = s_pc0.c
        cst_loaded = s_c.c
        res_ready = dma("sync", s_x, res[:].rearrange("p (k t) -> p k t", t=T),
                        xT[0].rearrange("(k p) t -> p k t", p=128))

        prev = [(s_c, cst_loaded)]

        def chain(emitfn, fn, drain=True):
            v = emitfn(fn, waits=list(prev), drain=drain)
            prev[:] = [(emitfn.__sem__, v)]
            return v

        pe.__sem__ = s_pe
        vec.__sem__ = s_v
        sca.__sem__ = s_s

        chain(vec, lambda h: h.tensor_scalar(
            out=lgb[:], in0=lgb[:], scalar1=1e-30, scalar2=None, op0=ALU.max))
        chain(sca, lambda h: h.activation(out=lgb[:], in_=lgb[:], func=AF.Ln))
        for h_i in range(H):
            pr, pc = (h_i % 2) * 64, (h_i // 2)
            chain(vec, lambda h, h_i=h_i: h.tensor_scalar(
                out=tf32[:], in0=etri_s[:], scalar1=lgb[:, h_i:h_i + 1],
                scalar2=None, op0=ALU.mult))
            chain(sca, lambda h: h.activation(out=tf32[:], in_=tf32[:], func=AF.Exp))
            dcol = (h_i // 8) * 1024 + (h_i % 2) * 512 + ((h_i % 8) // 2) * 128
            chain(vec, lambda h, dcol=dcol: h.tensor_tensor(
                out=dct[:, dcol:dcol + 128], in0=tf32[:], in1=tri_s[:],
                op=ALU.mult))
            chain(vec, lambda h, h_i=h_i, pr=pr: h.tensor_scalar(
                out=tf32[pr:pr + 64, :], in0=erow_s[pr:pr + 64, :],
                scalar1=lgb[pr:pr + 64, h_i:h_i + 1], scalar2=None, op0=ALU.mult))
            chain(sca, lambda h, pr=pr, pc=pc: h.activation(
                out=grw[pr:pr + 64, pc * 128:pc * 128 + 128],
                in_=tf32[pr:pr + 64, :], func=AF.Exp, bias=float(np.log(0.125))))
            chain(vec, lambda h, h_i=h_i: h.tensor_scalar(
                out=tf32[:, 0:64], in0=expc[:].broadcast_to([128, 64]),
                scalar1=lgb[:, h_i:h_i + 1], scalar2=None, op0=ALU.mult))
            chain(sca, lambda h, h_i=h_i: h.activation(
                out=kdc[:, h_i * 64:(h_i + 1) * 64], in_=tf32[:, 0:64], func=AF.Exp))
            chain(sca, lambda h, h_i=h_i, pr=pr, pc=pc: h.activation(
                out=gcs[pr:pr + 64, pc * 64:pc * 64 + 64],
                in_=lgb[pr:pr + 64, h_i:h_i + 1].broadcast_to([64, 64]),
                func=AF.Exp, scale=128.0))
        tables_done = list(prev)

        wslot = {"i": 0, "busy": [None, None], "sems": [s_w0, s_w1],
                 "q": ["gpsimd", "sync"]}

        def wstream(in_ap, ncols, extra=(), cast=False):
            sl = wslot["i"] % 2
            wslot["i"] += 1
            waits = list(extra)
            if wslot["busy"][sl] is not None:
                waits.append((s_pe, wslot["busy"][sl]))
            dst = wst[:, sl * 2048: sl * 2048 + ncols]
            if len(in_ap.shape) > 2:
                tail = list(in_ap.shape[1:])
                tail = tail[1:]
                d2 = dst
                if len(in_ap.shape) == 4:
                    d2 = dst.rearrange("p (a b c) -> p a b c",
                                       b=in_ap.shape[2], c=in_ap.shape[3])
                elif len(in_ap.shape) == 3:
                    d2 = dst.rearrange("p (a b) -> p a b", b=in_ap.shape[2])
                dst = d2
            sem = wslot["sems"][sl]
            q = "gpsimd" if cast else wslot["q"][sl]
            val = dma(q, sem, dst, in_ap, waits=waits)
            return sl, sl * 2048, (sem, val)

        def wslot_used(sl, pe_val):
            wslot["busy"][sl] = pe_val

        ln_cell = {"stats": None}

        def emit_ln_cast(k, t0, tn, waits=()):
            return vec(lambda h, k=k: h.tensor_copy(
                lnT[:, k * T + t0: k * T + t0 + tn],
                res[:, k * T + t0: k * T + t0 + tn]), waits=list(waits))

        def emit_ln(t0, tn, first_v_waits=(), use23=False, pre_casts=None,
                    sbf_free=None, v_fence=True, pab_extra=(), v_fence_val=None):
            SA = P2 if use23 else P0
            SB = P3 if use23 else P1
            abm = hbuf[:, 15360:15360 + tn]
            abm2 = hbuf[:, 14336:14336 + tn]
            """LN tokens [t0, t0+tn) of res -> lnT (bf16). tn in {256, 1024}."""
            ns = max(tn // 512, 1)
            nw = min(tn, 512)
            fence_v, fence_s = s_v.c, s_s.c
            prev_stats = ln_cell.get("stats")
            per_k = (isinstance(first_v_waits, list) and len(first_v_waits) == KC
                     and first_v_waits and isinstance(first_v_waits[0], list))
            cast_v, sq_s, st_pe = [], [], []
            for k in range(KC):
                if pre_casts is not None:
                    cv = pre_casts[k]
                else:
                    if per_k:
                        w = list(first_v_waits[k])
                    else:
                        w = list(first_v_waits) if k == 0 else []
                    cv = emit_ln_cast(k, t0, tn, w)
                cast_v.append(cv)
                par = k % 2
                sw = [(s_v, cv)]
                if k >= 2:
                    sw.append((s_pe, st_pe[k - 2]))
                else:
                    sw.append((s_v, fence_v if sbf_free is None else sbf_free))
                    if sbf_free is not None and prev_stats is not None:
                        sw.append((s_pe, prev_stats))
                sv = sca(lambda h, k=k, par=par: h.activation(
                    out=sbf[:, par * 1024: par * 1024 + tn],
                    in_=lnT[:, k * T + t0: k * T + t0 + tn], func=AF.Square),
                    waits=sw)
                sq_s.append(sv)
                pw = [(s_v, cv), (s_s, sv)]
                if k == 0:
                    pw += [(s_s, fence_s)]
                    if v_fence:
                        pw.append((s_v, fence_v if v_fence_val is None
                                   else v_fence_val))

                def mmfn(h, k=k, par=par):
                    last = None
                    for n in range(ns):
                        last = h.matmul(
                            SA[0:1, n * nw:(n + 1) * nw], onesb[:],
                            lnT[:, k * T + t0 + n * nw: k * T + t0 + (n + 1) * nw],
                            start=(k == 0), stop=(k == KC - 1))
                    for n in range(ns):
                        last = h.matmul(
                            SB[0:1, n * nw:(n + 1) * nw], onesb[:],
                            sbf[:, par * 1024 + n * nw: par * 1024 + (n + 1) * nw],
                            start=(k == 0), stop=(k == KC - 1))
                    return last

                pv = pe(mmfn, waits=pw)
                st_pe.append(pv)
            ln_cell["stats"] = st_pe[-1]
            sA = sca(lambda h: h.activation(out=rowb[0:1, 0:tn], in_=SA[0:1, 0:tn],
                                            func=AF.Square),
                     waits=[(s_pe, st_pe[-1])])
            v3 = vec(lambda h: h.tensor_tensor(
                out=SB[0:1, 0:tn], in0=SB[0:1, 0:tn], in1=rowb[0:1, 0:tn],
                op=ALU.subtract), waits=[(s_s, sA), (s_pe, st_pe[-1])], drain=True)
            sB = sca(lambda h: h.activation(out=rowb[0:1, 0:tn], in_=SB[0:1, 0:tn],
                                            func=AF.Ln, bias=1e-5),
                     waits=[(s_v, v3)], drain=True)
            pbc = ln_cell.get("bc")
            sC = sca(lambda h: h.activation(out=rowb[0:1, 0:tn],
                                            in_=rowb[0:1, 0:tn],
                                            func=AF.Exp, scale=-0.5), drain=True,
                     waits=[(s_c, pbc[0])] if pbc else [])
            vM = vec(lambda h: h.tensor_copy(rowb[0:1, 1024:1024 + tn],
                                             SA[0:1, 0:tn]),
                     waits=[(s_pe, st_pe[-1])]
                     + ([(s_c, pbc[1])] if pbc else []), drain=True)
            bw = [(s_v, ln_cell["app"])] if ln_cell.get("app") else []
            d1 = dma("gpsimd", s_c, lnrow_d[0:1, 0:tn], rowb[0:1, 0:tn],
                     waits=[(s_s, sC)] + ([(s_c, pbc[2])] if pbc else []))
            d2 = dma("gpsimd", s_c, lnrow_d[0:1, 1024:1024 + tn],
                     rowb[0:1, 1024:1024 + tn],
                     waits=[(s_v, vM)] + ([(s_c, pbc[3])] if pbc else []))
            bA = dma("scalar", s_c, abm,
                     lnrow_d[0:1, 0:tn].broadcast_to([128, tn]),
                     waits=[(s_c, d1)] + bw)
            bB = dma("scalar", s_c, abm2,
                     lnrow_d[0:1, 1024:1024 + tn].broadcast_to([128, tn]),
                     waits=[(s_c, d2)])
            ln_cell["bc"] = (d1, d2, bA, bB)
            app_v = []
            for k in range(KC):
                vec(lambda h, k=k: h.tensor_tensor(
                    out=lnT[:, k * T + t0:k * T + t0 + tn],
                    in0=lnT[:, k * T + t0:k * T + t0 + tn],
                    in1=abm2, op=ALU.subtract),
                    waits=[(s_c, bA), (s_c, bB)], drain=True, inc=False)
                v5 = vec(lambda h, k=k: h.tensor_tensor(
                    out=lnT[:, k * T + t0:k * T + t0 + tn],
                    in0=lnT[:, k * T + t0:k * T + t0 + tn],
                    in1=abm, op=ALU.mult), drain=True)
                app_v.append(v5)
            ln_cell["app"] = app_v[-1]
            return app_v

        def emit_gate_z(hf, ln_done, zP=None, extra_pe_waits=(),
                        extra_s_waits=(), extra_v_waits=()):
            zP = P0 if zP is None else zP
            t0 = hf * HALF
            pz = pe(lambda h: [h.matmul(
                zP[0:8, n * 512:(n + 1) * 512], gw_s[:, k * E:(k + 1) * E],
                lnT[:, k * T + t0 + n * 512: k * T + t0 + (n + 1) * 512],
                start=(k == 0), stop=(k == KC - 1))
                for k in range(KC) for n in range(2)][-1],
                waits=[(s_v, max(ln_done)), (s_g, gw_val)] + list(extra_pe_waits))
            sE = sca(lambda h: h.activation(out=rowb[0:8, 0:1024],
                                            in_=zP[0:8, 0:1024],
                                            func=AF.Exp, scale=-1.0),
                     waits=[(s_pe, pz)] + list(extra_s_waits))
            vec(lambda h: h.tensor_scalar(out=rowb[0:8, 0:1024],
                                          in0=rowb[0:8, 0:1024], scalar1=1.0,
                                          scalar2=None, op0=ALU.add),
                waits=[(s_s, sE)] + list(extra_v_waits), drain=True, inc=False)
            v2 = vec(lambda h: h.reciprocal(out=rowb[0:8, 1024:2048],
                                            in_=rowb[0:8, 0:1024]),
                     drain=True)
            return sE, v2

        def gate_tr1(hf, v2, extra=()):
            pt = pe(lambda h: [h.transpose(
                P3[:, t * 8:(t + 1) * 8],
                rowb[0:8, 1024 + t * 128: 1024 + (t + 1) * 128],
                idn_s[0:8, 0:8]) for t in range(8)][-1],
                    waits=[(s_v, v2)] + list(extra))
            v3 = vec(lambda h: h.tensor_copy(gsc[:, 0:64], P3[:, 0:64]),
                     waits=[(s_pe, pt)])

            def r3(a):
                return a.rearrange("p (t e) -> p t e", e=E)

            vec(lambda h: h.tensor_tensor(
                out=r3(gsc[:, 64:128]), in0=r3(gsc[:, 0:64]),
                in1=bcb[:, None, :].broadcast_to([128, 8, E]),
                op=ALU.add), drain=True, inc=False)
            vec(lambda h: h.tensor_reduce(
                out=gmx[:, 0:8], in_=r3(gsc[:, 64:128]), axis=AX.X, op=ALU.max),
                drain=True, inc=False)
            vec(lambda h: h.tensor_tensor(
                out=r3(gsc[:, 128:192]), in0=r3(gsc[:, 64:128]),
                in1=gmx[:, 0:8][:, :, None].broadcast_to([128, 8, E]),
                op=ALU.is_ge), drain=True, inc=False)
            vec(lambda h: h.scalar_tensor_tensor(
                out=r3(gsc[:, 192:256]), in0=r3(gsc[:, 128:192]), scalar=-1e30,
                in1=r3(gsc[:, 64:128]), op0=ALU.mult, op1=ALU.add),
                drain=True, inc=False)
            vec(lambda h: h.tensor_reduce(
                out=gmx[:, 8:16], in_=r3(gsc[:, 192:256]), axis=AX.X, op=ALU.max),
                drain=True, inc=False)
            vec(lambda h: h.tensor_tensor(
                out=r3(gsc[:, 128:192]), in0=r3(gsc[:, 64:128]),
                in1=gmx[:, 8:16][:, :, None].broadcast_to([128, 8, E]),
                op=ALU.is_ge), drain=True, inc=False)
            vA = vec(lambda h: h.scalar_tensor_tensor(
                out=gsc[:, 192:256], in0=gsc[:, 0:64], scalar=SQ5,
                in1=gsc[:, 128:192], op0=ALU.mult, op1=ALU.mult), drain=True)
            return vA, pt

        def gate_tr2(hf, vA):
            t0 = hf * HALF
            pt2 = pe(lambda h: [h.transpose(P3[0:8, t * 128:(t + 1) * 128],
                                            gsc[:, 192 + t * 8:192 + (t + 1) * 8],
                                            idn_s[:, :]) for t in range(8)][-1],
                     waits=[(s_v, vA)])
            vm = vec(lambda h, t0=t0: h.tensor_copy(mT[0:8, t0:t0 + HALF],
                                                    P3[0:8, 0:1024]),
                     waits=[(s_pe, pt2)])
            return vm

        def emit_ffn(first_v_waits=(), nexp=E, phase=3, tail_hook=None,
                     pre_casts=None, sbf_free=None, bf16_w=False,
                     first_w_extra=()):
            pc0 = pre_casts[0] if pre_casts else None
            pc1 = pre_casts[1] if pre_casts else None
            subl_fence = s_v.c
            wu_src0 = wub_d if bf16_w else wu_d
            pre_w = {}
            if phase >= 2:
                for m in (0, 2):
                    wex = first_w_extra if m == 0 else ()
                    pre_w[m] = wstream(
                        wu_src0[0, m:m + 2].rearrange("m p k c -> p m k c"),
                        2048, extra=wex, cast=not bf16_w)
            ln0 = emit_ln(0, HALF, first_v_waits=first_v_waits, pre_casts=pc0,
                          sbf_free=sbf_free)
            ln1 = emit_ln(HALF, HALF, use23=True, pre_casts=pc1,
                          pab_extra=[(s_v, subl_fence)], sbf_free=sbf_free,
                          v_fence_val=subl_fence)
            if phase == 0:
                return max(max(ln0), max(ln1))
            sE0, v20 = emit_gate_z(0, ln0 + [ln1[0]])
            pz1 = pe(lambda h: [h.matmul(
                P2[0:8, n * 512:(n + 1) * 512], gw_s[:, k * E:(k + 1) * E],
                lnT[:, k * T + HALF + n * 512: k * T + HALF + (n + 1) * 512],
                start=(k == 0), stop=(k == KC - 1))
                for k in range(KC) for n in range(2)][-1],
                waits=[(s_v, max(ln1)), (s_g, gw_val)])
            sE1 = sca(lambda h: h.activation(out=rowb[0:8, 0:1024],
                                             in_=P2[0:8, 0:1024],
                                             func=AF.Exp, scale=-1.0),
                      waits=[(s_pe, pz1), (s_v, v20)])
            gate_state = {}

            def g_tr1_0():
                vA0, pt0 = gate_tr1(0, v20, extra=[(s_v, max(ln1))])
                gate_state["vA0"] = vA0
                gate_state["pt0"] = pt0

            def g_tr2_0():
                gate_state["g0"] = gate_tr2(0, gate_state["vA0"])

            def g_sig1():
                vec(lambda h: h.tensor_scalar(out=rowb[0:8, 0:1024],
                                              in0=rowb[0:8, 0:1024], scalar1=1.0,
                                              scalar2=None, op0=ALU.add),
                    waits=[(s_s, sE1), (s_pe, gate_state["pt0"])],
                    drain=True, inc=False)
                gate_state["v21"] = vec(lambda h: h.reciprocal(
                    out=rowb[0:8, 1024:2048], in_=rowb[0:8, 0:1024]),
                    drain=True)

            def g_tr1_1():
                vA1, _pt1 = gate_tr1(1, gate_state["v21"],
                                     extra=[(s_v, gate_state["g0"])])
                gate_state["vA1"] = vA1

            def g_tr2_1():
                gate_state["g1"] = gate_tr2(1, gate_state["vA1"])

            gate_steps = {1: g_tr1_0, 4: g_tr2_0, 5: g_sig1,
                          7: g_tr1_1, 10: g_tr2_1}
            ln_done = max(max(ln0), max(ln1))
            if phase == 1:
                for mm in sorted(gate_steps):
                    gate_steps[mm]()
                return max(gate_state["g0"], gate_state["g1"])

            up_free = {}
            mbc_free = None
            dn_free = {}
            last_vd = None
            for e in range(nexp):
                prev_down_pe = s_pe.c
                wd_val = None
                for hf in range(2):
                    t0 = hf * HALF
                    sq_done = []
                    wu_cols = {}
                    for m in range(FC):
                        if m % 2 == 0:
                            wu_src = wub_d if bf16_w else wu_d
                            wex = (first_w_extra if (e == 0 and hf == 0 and m == 0)
                                   else ())
                            if e == 0 and hf == 0 and m in pre_w:
                                sl, cols, wuv = pre_w.pop(m)
                            else:
                                sl, cols, wuv = wstream(
                                    wu_src[e, m:m + 2].rearrange(
                                        "m p k c -> p m k c"),
                                    2048, extra=wex, cast=not bf16_w)
                            wu_cols = {"sl": sl, "cols": cols, "val": wuv}
                            if hf == 0 and m in (6, 8, 10, 12):
                                q = (m - 6) // 2
                                wdw = [(s_pe, prev_down_pe)]
                                wd_src = wdb_d if bf16_w else wd_d
                                wd_val = dma(
                                    "gpsimd", s_wd,
                                    wdbuf[:, q * 4 * D:(q + 1) * 4 * D].rearrange(
                                        "p (k c) -> p k c", c=D),
                                    wd_src[e, q * 4:(q + 1) * 4].rearrange(
                                        "k p c -> p k c"), waits=wdw)
                        par = m % 2
                        PU = P0 if par == 0 else P1
                        base = wu_cols["cols"] + (m % 2) * 1024
                        mmw = [wu_cols["val"], (s_v, ln_done)]
                        if e == 0 and hf == 0 and m == 0:
                            mmw.append((s_s, sE0))
                        if par in up_free:
                            mmw.append((s_s, up_free[par]))
                        pu = pe(lambda h, PU=PU, base=base, t0=t0: [h.matmul(
                            PU[:, n * 512:(n + 1) * 512],
                            wst[:, base + k * 128: base + (k + 1) * 128],
                            lnT[:, k * T + t0 + n * 512: k * T + t0 + (n + 1) * 512],
                            start=(k == 0), stop=(k == KC - 1))
                            for k in range(KC) for n in range(2)][-1],
                            waits=mmw)
                        if m % 2 == 1:
                            wslot_used(wu_cols["sl"], pu)
                        r1c = (m % 3) * 1024
                        srw = [(s_pe, pu)]
                        if len(sq_done) >= 3:
                            srw.append((s_v, sq_done[-3]))
                        sr = sca(lambda h, PU=PU, r1c=r1c: h.activation(
                            out=sbf[:, r1c:r1c + 1024], in_=PU[:, 0:1024],
                            func=AF.Relu), waits=srw)
                        sq = vec(lambda h, m=m, r1c=r1c: h.tensor_tensor(
                            out=hbuf[:, m * HALF:(m + 1) * HALF],
                            in0=sbf[:, r1c:r1c + 1024],
                            in1=sbf[:, r1c:r1c + 1024], op=ALU.mult),
                            waits=[(s_s, sr)])
                        up_free[par] = sr
                        sq_done.append(sq)
                        if e == 0 and hf == 0 and m in gate_steps:
                            gate_steps[m]()
                    if phase == 2:
                        last_vd = vmb if last_vd is None else last_vd
                        continue
                    gv = gate_state["g0"] if hf == 0 else gate_state["g1"]
                    pmw = [(s_v, gv)]
                    if 1 in up_free:
                        pmw.append((s_s, up_free[1]))
                    if mbc_free is not None:
                        pmw.append((s_v, mbc_free))
                    pm = pe(lambda h, e=e, t0=t0: [h.matmul(
                        P1[:, n * 512:(n + 1) * 512],
                        sel_s[:, e * 128:(e + 1) * 128],
                        mT[0:8, t0 + n * 512: t0 + (n + 1) * 512],
                        start=True, stop=True) for n in range(2)][-1],
                        waits=pmw)
                    smb = sca(lambda h: h.activation(out=mbc[:], in_=P1[:, 0:1024],
                                                     func=AF.Square),
                              waits=[(s_pe, pm)])
                    for md in range(KC):
                        par = md % 2
                        PD = P2 if par == 0 else P3
                        mmw = [(s_v, sq_done[11] if md == 0 else sq_done[-1])]
                        if md < 2:
                            mmw += [(s_wd, wd_val), (s_v, gv)]
                        if par in dn_free:
                            mmw.append((s_v, dn_free[par]))
                        ksplit = 12 if md == 0 else FC
                        pd = pe(lambda h, md=md, PD=PD, t0=t0, ks=ksplit: [h.matmul(
                            PD[:, n * 512:(n + 1) * 512],
                            wdbuf[:, k * D + md * 128: k * D + md * 128 + 128],
                            hbuf[:, k * HALF + n * 512: k * HALF + (n + 1) * 512],
                            start=(k == 0), stop=(k == FC - 1))
                            for k in range(ks) for n in range(2)][-1],
                            waits=mmw)
                        if md == 0:
                            pd = pe(lambda h, md=md, PD=PD, t0=t0: [h.matmul(
                                PD[:, n * 512:(n + 1) * 512],
                                wdbuf[:, k * D + md * 128: k * D + md * 128 + 128],
                                hbuf[:, k * HALF + n * 512: k * HALF + (n + 1) * 512],
                                start=(k == 0), stop=(k == FC - 1))
                                for k in range(12, FC) for n in range(2)][-1],
                                waits=[(s_v, sq_done[-1])])
                        vec(lambda h, PD=PD: h.tensor_tensor(
                            out=PD[:, 0:1024], in0=PD[:, 0:1024], in1=mbc[:],
                            op=ALU.mult), waits=[(s_pe, pd), (s_s, smb)],
                            inc=False)
                        vd = vec(lambda h, md=md, PD=PD, t0=t0: h.scalar_tensor_tensor(
                            out=res[:, md * T + t0: md * T + t0 + 1024],
                            in0=PD[:, 0:1024], scalar=1.0,
                            in1=res[:, md * T + t0: md * T + t0 + 1024],
                            op0=ALU.mult, op1=ALU.add), drain=True,
                            waits=[])
                        dn_free[par] = vd
                        last_vd = vd
                        mbc_free = vd
                        if tail_hook is not None and e == nexp - 1:
                            tail_hook(hf, md, vd)
                    emit_ffn.sbf_last = sq_done[-1]
            return last_vd

        def emit_attn(first_v_waits=(), phase=4, ngrp=NGRP,
                      tail_hook=None, pre_casts=None, sbf_free=None,
                      bf16_w=False, group_hook=None, first_w_extra=()):
            wq_src = wqb_d if bf16_w else wq_d
            wk_src = wkb_d if bf16_w else wk_d
            wq_pv, wk_pv = [], []
            for q in range(4):
                gw_ = ([(s_pe, s_pe.c)] + list(first_w_extra)) if q == 0 else []
                wq_pv.append(dma(
                    "sync" if bf16_w else "gpsimd", s_wd,
                    wdbuf[:, q * 2048:(q + 1) * 2048].rearrange(
                        "p (m c) -> p m c", c=1024),
                    wq_src[2 * q:2 * q + 2].rearrange("m p c -> p m c"),
                    waits=gw_))
            for q in range(4):
                gw_ = ([(s_pe, s_pe.c)] + list(first_w_extra)) if q == 0 else []
                wk_pv.append(dma(
                    "gpsimd", s_wk,
                    wdbuf[:, 8192 + q * 2048:8192 + (q + 1) * 2048].rearrange(
                        "p (m c) -> p m c", c=1024),
                    wk_src[2 * q:2 * q + 2].rearrange("m p c -> p m c"),
                    waits=gw_))
            wkc_val = wk_pv[-1]
            vec(lambda h: h.memset(S_all[:], 0.0), waits=[(s_v, s_v.c)], inc=False)
            sbf_ready = vec(lambda h: h.memset(S_bf[:], 0.0))
            p23_free = sbf_ready
            slot_free = {}
            SLOTS = [(P0, 0), (P0, 512), (P1, 0), (P1, 512)]
            sl_i = {"i": 0}

            def proj_slot():
                i = sl_i["i"] % 4
                sl_i["i"] += 1
                return i, SLOTS[i]

            ret_done = None
            pending = []
            scev_cell = {"v": None}

            def emit_oproj(t0, ret_gate, hook=None):
                rd = None
                for m0 in range(0, KC, 2):
                    sl, cols, wov = wstream(
                        wob_d[m0:m0 + 2].rearrange("m p c -> p m c"), 2048,
                        extra=[(s_pc0, wob_val)] if m0 == 0 else ())
                    last_po = None
                    for mi in range(2):
                        m = m0 + mi
                        si, (PT, off) = proj_slot()
                        mmw = [wov, (s_v, ret_gate)]
                        if si in slot_free:
                            mmw.append((s_v, slot_free[si]))
                        po = pe(lambda h, cols=cols, mi=mi, PT=PT, off=off: [h.matmul(
                            PT[:, off:off + GRP],
                            wst[:, cols + mi * 1024 + k * 128:
                                cols + mi * 1024 + (k + 1) * 128],
                            hbuf[:, A_RET + k * GRP: A_RET + (k + 1) * GRP],
                            start=(k == 0), stop=(k == KC - 1))
                            for k in range(KC)][-1], waits=mmw)
                        last_po = po
                        vo = vec(lambda h, PT=PT, off=off, m=m, t0=t0:
                                 h.scalar_tensor_tensor(
                                     out=res[:, m * T + t0: m * T + t0 + GRP],
                                     in0=PT[:, off:off + GRP], scalar=1.0,
                                     in1=res[:, m * T + t0: m * T + t0 + GRP],
                                     op0=ALU.mult, op1=ALU.add), waits=[(s_pe, po)])
                        slot_free[si] = vo
                        rd = vo
                        if hook is not None:
                            hook(m, vo)
                    wslot_used(sl, last_po)
                return rd

            pr_prev, pst_prev = s_pe.c, s_pe.c
            pc0 = pre_casts[0] if pre_casts else None
            pc1 = pre_casts[1] if pre_casts else None
            attn_fence = s_v.c
            lnA = emit_ln(0, HALF, first_v_waits=first_v_waits, pre_casts=pc0,
                          sbf_free=sbf_free, v_fence=False)
            lnB = emit_ln(HALF, HALF, use23=True, pre_casts=pc1,
                          sbf_free=sbf_free,
                          pab_extra=[(s_v, attn_fence)],
                          v_fence_val=attn_fence)
            ln_done = max(max(lnA), max(lnB))
            for g in range(ngrp):
                t0 = g * GRP
                if group_hook is not None:
                    group_hook(g)
                if phase == 0:
                    ret_done = ln_done
                    continue
                if pending:
                    pg, pt0, prg = pending.pop(0)
                    hook = None
                    if tail_hook is not None and pg in (3, ngrp - 1):
                        hook = (lambda m, vo, pg=pg: tail_hook(
                            0 if pg == 3 else 1, m, vo))
                    ret_done = emit_oproj(pt0, prg, hook=hook)
                qkT_ev = []
                for m in range(KC):
                    for coff, dst in ((0, A_QT), (8192, A_KT)):
                        si, (PT, off) = proj_slot()
                        wsem = (s_wd, wq_pv[m // 2]) if coff == 0 \
                            else (s_wk, wk_pv[m // 2])
                        mmw = [wsem, (s_v, ln_done)]
                        if si in slot_free:
                            mmw.append((s_v, slot_free[si]))
                        pq = pe(lambda h, coff=coff, m=m, PT=PT, off=off,
                                t0=t0: [h.matmul(
                                    PT[:, off:off + GRP],
                                    wdbuf[:, coff + m * 1024 + k * 128:
                                          coff + m * 1024 + (k + 1) * 128],
                                    lnT[:, k * T + t0: k * T + t0 + GRP],
                                    start=(k == 0), stop=(k == KC - 1))
                                    for k in range(KC)][-1], waits=mmw)
                        vq = vec(lambda h, PT=PT, off=off, dst=dst, m=m:
                                 h.tensor_copy(
                                     hbuf[:, dst + m * GRP: dst + (m + 1) * GRP],
                                     PT[:, off:off + GRP]), waits=[(s_pe, pq)])
                        slot_free[si] = vq
                        qkT_ev.append(vq)
                qkT_done = max(qkT_ev)
                tokev = []
                for c in range(2):
                    PT2 = P2 if c == 0 else P3
                    tw = ([(s_v, qkT_ev[-1]),
                           (s_v, s_v.c if g == 0 else p23_free)]
                          if c == 0 else [])
                    ptr = pe(lambda h, c=c, PT2=PT2: [h.matmul(
                        PT2[:, m * 128:(m + 1) * 128],
                        hbuf[:, A_KT + m * GRP + c * 128:
                             A_KT + m * GRP + c * 128 + 128],
                        idnb_s[:], start=True, stop=True)
                        for m in range(KC)][-1], waits=tw)
                    vv = vec(lambda h, c=c, PT2=PT2: h.tensor_copy(
                        hbuf[:, A_KTOK + c * D: A_KTOK + (c + 1) * D],
                        PT2[:, 0:1024]), waits=[(s_pe, ptr)])
                    tokev.append(vv)
                mm_last = None
                for k0 in range(0, KC, 2):
                    sl, cols, wkv = wstream(
                        wvb_d[k0:k0 + 2].rearrange("k p c -> p k c"), 2048,
                        extra=[(s_pc0, wvb_val)] if (g == 0 and k0 == 0) else ())
                    last_pv = None
                    for ki in range(2):
                        k = k0 + ki
                        mmw = [wkv, (s_v, ln_done)]
                        if k == 0:
                            mmw.append((s_v, max(tokev)))
                        pv = pe(lambda h, k=k, cols=cols, ki=ki, t0=t0: [h.matmul(
                            (P2 if t == 0 else P3)[:, n * 512:(n + 1) * 512],
                            lnT[:, k * T + t0 + t * 128:
                                k * T + t0 + t * 128 + 128],
                            wst[:, cols + ki * 1024 + n * 512:
                                cols + ki * 1024 + (n + 1) * 512],
                            start=(k == 0), stop=(k == KC - 1))
                            for t in range(2) for n in range(2)][-1],
                            waits=mmw)
                        last_pv = pv
                        mm_last = pv
                    wslot_used(sl, last_pv)
                for t in range(2):
                    vv = vec(lambda h, t=t: h.tensor_copy(
                        hbuf[:, A_V + t * D: A_V + (t + 1) * D],
                        (P2 if t == 0 else P3)[:, 0:1024]),
                        waits=[(s_pe, mm_last)])
                    tokev.append(vv)
                tok_done = max(tokev)
                if phase == 2:
                    ret_done = tok_done
                    continue
                QP_OFF = (A_QP, 14336)
                KP_OFF = (A_KP, 15360)
                sc_pre = []
                for c in range(2):
                    ct = c * 128
                    w = [(s_v, qkT_done)]
                    if c == 0:
                        w += [(s_pe, pr_prev), (s_pe, pst_prev)]
                    vqp = vec(lambda h, ct=ct, c=c: h.tensor_tensor(
                        out=hbuf[:, QP_OFF[c]:QP_OFF[c] + 1024].rearrange(
                            "p (k c2) -> p k c2", c2=128),
                        in0=hbuf[:, A_QT:A_QT + 2048].rearrange(
                            "p (k c2) -> p k c2", c2=GRP)[:, :, ct:ct + 128],
                        in1=grw[:].rearrange("p (k c2) -> p k c2", c2=128),
                        op=ALU.mult), waits=w)
                    vkp = vec(lambda h, c=c: h.tensor_tensor(
                        out=hbuf[:, KP_OFF[c]:KP_OFF[c] + 1024],
                        in0=hbuf[:, A_KTOK + c * D: A_KTOK + (c + 1) * D],
                        in1=kdc[:], op=ALU.mult),
                        waits=[(s_v, tok_done)])
                    sc_pre.append((vqp, vkp))
                ret_ev = []
                sub = ATTN_SUB
                for c in range(2):
                    ct = c * 128
                    vqp, vkp = sc_pre[c]
                    if sub == 0:
                        ret_ev = [vqp, vkp]
                        continue
                    sc_ev = []
                    for hs in range(2):
                        mmw = [(s_v, qkT_done)]
                        if hs == 0 and c == 0:
                            mmw.append((s_v, tok_done))
                        if hs == 0 and scev_cell["v"] is not None:
                            mmw.append((s_v, scev_cell["v"]))
                        if sc_ev:
                            mmw.append((s_v, sc_ev[-1]))
                        ps = pe(lambda h, hs=hs, ct=ct: [h.matmul(
                            P2[:, sccol(h_i):sccol(h_i) + 128],
                            hbuf[:, A_KT + (h_i // 2) * GRP + ct:
                                 A_KT + (h_i // 2) * GRP + ct + 128][
                                (h_i % 2) * 64:(h_i % 2) * 64 + 64, :],
                            hbuf[:, A_QT + (h_i // 2) * GRP + ct:
                                 A_QT + (h_i // 2) * GRP + ct + 128][
                                (h_i % 2) * 64:(h_i % 2) * 64 + 64, :],
                            start=True, stop=True)
                            for h_i in range(hs * 8, hs * 8 + 8)][-1],
                            waits=mmw)
                        vsc = vec(lambda h, hs=hs: h.tensor_tensor(
                            out=hbuf[:, A_SC + hs * 1024: A_SC + (hs + 1) * 1024],
                            in0=P2[:, 0:1024],
                            in1=dct[:, hs * 1024:(hs + 1) * 1024], op=ALU.mult),
                            waits=[(s_pe, ps)])
                        sc_ev.append(vsc)
                    scev_cell["v"] = sc_ev[-1]
                    if sub == 1:
                        ret_ev = [max(sc_ev)]
                        continue
                    if sub >= 4:
                        pst = pe(lambda h, c=c: [h.matmul(
                            P2[(h_i % 2) * 64:(h_i % 2) * 64 + 64,
                               (h_i // 2) * 64:(h_i // 2) * 64 + 64],
                            hbuf[:, KP_OFF[c] + h_i * 64: KP_OFF[c] + h_i * 64 + 64],
                            hbuf[:, A_V + c * D + h_i * 64:
                                 A_V + c * D + h_i * 64 + 64],
                            start=True, stop=True,
                            tile_position=(0, (h_i % 2) * 64)) for h_i in range(H)][-1],
                            waits=[(s_v, max(vkp, sc_ev[-1]))])
                        pst_prev = pst
                        vec(lambda h: h.tensor_tensor(
                            out=S_all[:], in0=S_all[:], in1=gcs[:], op=ALU.mult),
                            drain=True, inc=False)
                        s_add = vec(lambda h: h.tensor_tensor(
                            out=S_all[:], in0=S_all[:], in1=P2[:, 0:512],
                            op=ALU.add), waits=[(s_pe, pst)], drain=True)
                        scev_cell["v"] = max(scev_cell["v"], s_add)
                        nxt_sbf = vec(lambda h, c=c: h.tensor_copy(
                            (S_bf2 if c == 0 else S_bf)[:], S_all[:]),
                            drain=True)
                    mmw = [(s_v, max(sc_ev[-1], vqp, sbf_ready))]
                    if c == 0:
                        mmw.append((s_v, tok_done))
                    if ret_ev:
                        mmw.append((s_v, ret_ev[-1]))
                    def retfn(h, c=c, sub=sub):
                        last = None
                        for h_i in range(H):
                            rr = (h_i % 2) * 64
                            pcol = (h_i // 2)
                            mm1 = h.matmul(
                                P3[rr:rr + 64, pcol * 128:pcol * 128 + 128],
                                hbuf[:, A_V + c * D + h_i * 64:
                                     A_V + c * D + h_i * 64 + 64],
                                hbuf[:, A_SC + (h_i // 8) * 1024 + sccol(h_i):
                                     A_SC + (h_i // 8) * 1024 + sccol(h_i) + 128],
                                start=True, stop=(sub == 2), tile_position=(0, rr))
                            if sub == 2:
                                last = mm1
                                continue
                            last = h.matmul(
                                P3[rr:rr + 64, pcol * 128:pcol * 128 + 128],
                                (S_bf if c == 0 else S_bf2)[
                                    rr:rr + 64, pcol * 64:pcol * 64 + 64],
                                hbuf[:, QP_OFF[c] + pcol * 128:
                                     QP_OFF[c] + pcol * 128 + 128][rr:rr + 64, :],
                                start=False, stop=True, tile_position=(rr, rr))
                        return last

                    pr = pe(retfn, waits=mmw)
                    pr_prev = pr
                    vre = vec(lambda h, ct=ct: h.tensor_copy(
                        hbuf[:, A_RET:A_RET + 2048].rearrange(
                            "p (k c2) -> p k c2", c2=GRP)[:, :, ct:ct + 128],
                        P3[:, 0:1024].rearrange("p (k c2) -> p k c2", c2=128)),
                        waits=[(s_pe, pr)])
                    ret_ev.append(vre)
                    if sub == 3:
                        continue
                    sbf_ready = nxt_sbf
                    p23_free = vre
                if phase == 3:
                    ret_done = max(ret_ev)
                    continue
                pending.append((g, t0, max(ret_ev)))
            emit_attn.sbf_last = ln_done
            while pending:
                pg, pt0, prg = pending.pop(0)
                hook = None
                if tail_hook is not None and pg in (3, ngrp - 1):
                    hook = (lambda m, vo, pg=pg: tail_hook(0 if pg == 3 else 1,
                                                           m, vo))
                ret_done = emit_oproj(pt0, prg, hook=hook)
            return ret_done

        last_store = 0
        resv_k = [(s_x, res_ready)] * KC
        for b in range(n_batches):
            done = s_v.c
            fw = [[w] for w in resv_k]
            full = sublayers == (0, 1, 2) and FFN_PHASE == 3 and ATTN_PHASE == 4
            attn_pc = [[None] * KC, [None] * KC]
            ffn2_pc = [[None] * KC, [None] * KC]

            def ffn1_hook(hf, md, vd):
                attn_pc[hf][md] = emit_ln_cast(md, hf * HALF, HALF,
                                               waits=[(s_v, vd)])

            def attn_hook(hf, m, vo):
                ffn2_pc[hf][m] = emit_ln_cast(m, hf * HALF, HALF,
                                              waits=[(s_v, vo)])

            next_resv = [None] * KC

            def ffn2_hook(hf, md, vd, b=b):
                if hf != 1:
                    return
                vfk = vec(lambda h: h.tensor_copy(gsc[:, 8:16], gsc[:, 8:16]),
                          waits=[(s_v, vd)], drain=True)
                st = dma("sync", s_ch[md],
                         out_d[b].rearrange("(k p) t -> k p t", p=128)[md],
                         res[:, md * T:(md + 1) * T], waits=[(s_v, vfk)])
                if b + 1 < n_batches:
                    ld = dma("sync", s_ch[md], res[:, md * T:(md + 1) * T],
                             xT[b + 1].rearrange("(k p) t -> k p t", p=128)[md],
                             waits=[(s_ch[md], st)])
                    next_resv[md] = (s_ch[md], ld)

            def cast_hook(g):
                if g < E:
                    dma("gpsimd", s_pc, wub_d[g, 0:8], wu_d[g, 0:8])
                    dma("gpsimd", s_pc, wub_d[g, 8:16], wu_d[g, 8:16])
                    dma("gpsimd", s_pc, wdb_d[g, 0:8], wd_d[g, 0:8])
                    dma("gpsimd", s_pc, wdb_d[g, 8:16], wd_d[g, 8:16])
                if g == 6:
                    dma("gpsimd", s_pc, wqb_d, wq_d)
                    dma("gpsimd", s_pc, wkb_d, wk_d)

            bf16w = full and b > 0
            if 0 in sublayers:
                done = emit_ffn(first_v_waits=fw, nexp=FFN_NEXP, phase=FFN_PHASE,
                                tail_hook=ffn1_hook if full else None,
                                bf16_w=bf16w,
                                first_w_extra=[(s_pc, s_pc.c)] if bf16w else ())
                fw = []
            if 1 in sublayers:
                done = emit_attn(first_v_waits=fw, phase=ATTN_PHASE,
                                 ngrp=ATTN_NGRP,
                                 tail_hook=attn_hook if full else None,
                                 pre_casts=attn_pc if full else None,
                                 sbf_free=getattr(emit_ffn, "sbf_last", None)
                                 if full else None,
                                 bf16_w=bf16w,
                                 group_hook=cast_hook if (full and b == 0)
                                 else None,
                                 first_w_extra=[(s_pc, s_pc.c)] if bf16w else ())
                fw = []
            if 2 in sublayers:
                done = emit_ffn(first_v_waits=fw,
                                pre_casts=ffn2_pc if full else None,
                                sbf_free=getattr(emit_attn, "sbf_last", None)
                                if full else None,
                                bf16_w=(full and True),
                                first_w_extra=[(s_pc, s_pc.c)]
                                if (full and b == 0) else (),
                                tail_hook=ffn2_hook if full else None)
                fw = []
            if full:
                if b + 1 < n_batches:
                    assert all(v is not None for v in next_resv)
                    resv_k = next_resv
            else:
                vf = vec(lambda h: h.tensor_copy(gsc[:, 0:8], gsc[:, 0:8]),
                         waits=[(s_v, done)], drain=True)
                new_resv = []
                st_vals = []
                for k in range(KC):
                    st_k = dma("sync", s_ch[k],
                               out_d[b].rearrange("(k p) t -> k p t", p=128)[k],
                               res[:, k * T:(k + 1) * T], waits=[(s_v, vf)])
                    st_vals.append(st_k)
                for k in range(KC):
                    if b + 1 < n_batches:
                        ld_k = dma("sync", s_ch[k], res[:, k * T:(k + 1) * T],
                                   xT[b + 1].rearrange("(k p) t -> k p t",
                                                       p=128)[k],
                                   waits=[(s_ch[k], st_vals[k])])
                        new_resv.append((s_ch[k], ld_k))
                if new_resv:
                    resv_k = new_resv
        fin = s_x.c
        fin_ch = [s.c for s in s_ch]

        def _(h, fin=fin, fin_ch=tuple(fin_ch), sems=tuple(s.h for s in s_ch)):
            h.wait_ge(s_x.h, fin)
            for sh, fv in zip(sems, fin_ch):
                h.wait_ge(sh, fv)

        em.add("sync", _)
        em.flush()
    nc.finalize()
    return nc


def _prepack(inputs):
    x = np.asarray(inputs["L0_attn_x"], dtype=np.float32)
    wq = np.asarray(inputs["L0_attn_wq"], dtype=np.float32)
    wk = np.asarray(inputs["L0_attn_wk"], dtype=np.float32)
    wv = np.asarray(inputs["L0_attn_wv"], dtype=np.float32)
    wo = np.asarray(inputs["L0_attn_wo"], dtype=np.float32)
    gam = np.asarray(inputs["L0_attn_gammas"], dtype=np.float32)
    gw = np.asarray(inputs["L0_ffn_gate_w"], dtype=np.float32)
    wu = np.asarray(inputs["L0_ffn_w_up"], dtype=np.float32)
    wd = np.asarray(inputs["L0_ffn_w_down"], dtype=np.float32)
    eb = np.asarray(inputs["L0_ffn_expert_bias"], dtype=np.float32)

    def mmaj(w):
        return np.ascontiguousarray(
            w.reshape(KC, 128, KC, 128).transpose(0, 3, 2, 1).reshape(KC, 128, D))

    def kmaj(w):
        return np.ascontiguousarray(w.reshape(D, KC, 128).transpose(1, 2, 0))

    packs = {
        "wu_p": np.ascontiguousarray(
            wu.reshape(E, FC, 128, KC, 128).transpose(0, 1, 4, 3, 2)),
        "wd_p": np.ascontiguousarray(
            wd.reshape(E, D, FC, 128).transpose(0, 2, 3, 1)),
        "wq_m": mmaj(wq), "wk_m": mmaj(wk), "wo_m": mmaj(wo),
        "wk_k": kmaj(wk), "wv_k": kmaj(wv),
        "gw_p": np.ascontiguousarray(gw.reshape(E, KC, 128).transpose(1, 2, 0)),
        "ebias": np.ascontiguousarray(eb.reshape(1, E)),
        "gam": np.ascontiguousarray(gam.reshape(1, H)),
    }
    tau = np.arange(128, dtype=np.float32)
    consts = {
        "etri": np.maximum(tau[None, :] - tau[:, None], 0.0).astype(ml_dtypes.bfloat16),
        "tri": ((tau[None, :] >= tau[:, None]) * 0.125).astype(ml_dtypes.bfloat16),
        "erow": np.tile(tau[None, :] + 1.0, (128, 1)).astype(ml_dtypes.bfloat16),
        "ecol": np.ascontiguousarray((127.0 - tau).reshape(128, 1)),
        "idn": np.eye(128, dtype=np.float32),
        "idnb": np.eye(128, dtype=ml_dtypes.bfloat16),
        "ones1": np.ones((1, 128), dtype=np.float32),
        "onesb": np.full((128, 1), 1.0 / D, dtype=ml_dtypes.bfloat16),
        "sel_p": np.ascontiguousarray(
            np.eye(E, dtype=ml_dtypes.bfloat16)[:, :, None].repeat(128, 2)),
        "cvals": np.array([[1e-5, np.log(0.125), 1.0, 0, 0, 0, 0, 0]], np.float32),
    }
    xt = np.ascontiguousarray(x.reshape(NCORE, BPC, T, D).transpose(0, 1, 3, 2))
    return xt, packs, consts


def kernel(**inputs):
    xt, packs, consts = _prepack(inputs)
    nc = build()
    in_maps = []
    for c in range(NCORE):
        m = {"xT": xt[c]}
        m.update(packs)
        m.update(consts)
        in_maps.append(m)
    res = run_bass_kernel_spmd(nc, in_maps, core_ids=list(range(NCORE)))
    out = np.stack([r["outT"] for r in res.results])
    out = out.reshape(NCORE * BPC, D, T).transpose(0, 2, 1)
    return np.ascontiguousarray(out)
